# version 10
# baseline (speedup 1.0000x reference)
import numpy as np
import ml_dtypes
import concourse.bass as bass
import concourse.mybir as mybir
from concourse.bass_utils import run_bass_kernel_spmd
from contextlib import ExitStack

F32 = mybir.dt.float32
BF16 = mybir.dt.bfloat16
I32 = mybir.dt.int32
AF = mybir.ActivationFunctionType
ALU = mybir.AluOpType
AX = mybir.AxisListType

B, L, DM = 4, 4096, 4096
NH, CD, NB = 16, 32, 8
HD = CD * NB
NCORES = 8
HPC = NH // NCORES
FS = HPC * HD
T1 = 256
NT1 = (B * L) // T1
CPB = L // T1
NPAIR = B * HPC
LCH = L // 128
NLC = NPAIR * LCH
EPS = 1e-5

_ENGINES = ("sync", "gpsimd", "vector", "scalar", "tensor")


def _tables():
    g = [1.0, 1.0, 1.0]
    dim, n = 3, 8
    masks = sorted(range(n), key=lambda m: (bin(m).count("1"), m))
    inv = {m: i for i, m in enumerate(masks)}
    C = np.zeros((n, n, n), dtype=np.float64)
    for i, a in enumerate(masks):
        for j, b_ in enumerate(masks):
            sw, at = 0, a >> 1
            while at:
                sw += bin(at & b_).count("1")
                at >>= 1
            s = -1.0 if sw & 1 else 1.0
            for t in range(dim):
                if a >> t & 1 and b_ >> t & 1:
                    s *= g[t]
            C[i, j, inv[a ^ b_]] = s
    grades = np.array([bin(m).count("1") for m in masks])
    rev = np.where(grades * (grades - 1) // 2 % 2 == 1, -1.0, 1.0)
    met = np.array([np.prod([g[t] for t in range(dim) if m >> t & 1] + [1.0])
                    for m in masks])
    return C, rev, met


_C_TAB, _REV, _MET = _tables()
_BIV_IDX = [4, 5, 6]
_SCALE = 1.0 / np.sqrt(CD * NB)

_BIVKT = np.ascontiguousarray(
    _C_TAB[:, :, _BIV_IDX].transpose(2, 0, 1).reshape(3, 64)).astype(np.float32)
_REV64 = np.repeat(_REV, 8).reshape(64, 1).astype(np.float32)
_D64 = np.zeros((64, 1), dtype=np.float64)
for _i in range(8):
    _D64[_i * 8 + _i, 0] = _REV[_i] * _MET[_i] * _SCALE
_D64 = _D64.astype(np.float32)
_IDENT = np.eye(128, dtype=np.float32).astype(ml_dtypes.bfloat16)
_SHIFTS = (4 * (np.arange(128) % 8)).astype(np.int32).reshape(128, 1)


class Sem:
    __slots__ = ("h", "n", "name")

    def __init__(self, h, name=""):
        self.h = h
        self.n = 0
        self.name = name


class Phase:

    def __init__(self, name):
        self.name = name
        self.ops = {e: [] for e in _ENGINES}
        self._waits = []

    def add(self, eng, fn, waits=(), incs=(), cincs=(), rb=None):
        waits = tuple(waits)
        incs = tuple(incs)
        cincs = tuple(cincs)
        assert len(incs) <= 1, "one hardware sem update per instruction"
        for s, amt in incs + cincs:
            s.n += amt
        self._waits.extend(waits)
        self.ops[eng].append((waits, fn, incs, cincs, rb))

    def run(self, nc, carriers=None):
        for s, v in self._waits:
            assert v <= s.n, f"{self.name}: wait {v} > final count {s.n} on {s.name}"
        with nc.Block() as blk:
            for eng_name in _ENGINES:
                if not self.ops[eng_name]:
                    continue

                def body(e, _eng=eng_name, _ops=self.ops[eng_name]):
                    for waits, fn, incs, cincs, rb in _ops:
                        for s, v in waits:
                            e.wait_ge(s.h, v)
                        inst = fn(e)
                        if incs:
                            assert inst is not None
                            inst.then_inc(incs[0][0].h, incs[0][1])
                        for s, amt in cincs:
                            if _eng == "vector":
                                e.memset(carriers[0][:, :], 0.0).then_inc(
                                    s.h, amt)
                            elif _eng == "scalar":
                                e.drain().then_inc(s.h, amt)
                            elif _eng == "tensor":
                                raise AssertionError(
                                    "no safe PE carrier; restructure")
                            else:
                                e.nop().then_inc(s.h, amt)

                getattr(blk, eng_name)(body)


def build_program(nphases=3, npair=NPAIR, nlc_cap=None, debug=False, p1_cap=None):
    nc = bass.Bass("TRN2", target_bir_lowering=False)

    x_d = nc.dram_tensor("x", [B, L, DM], F32, kind="ExternalInput").ap()
    xres_d = nc.dram_tensor("xres", [B, L, FS], F32, kind="ExternalInput").ap()
    wq_d = nc.dram_tensor("wq", [FS, DM], F32, kind="ExternalInput").ap()
    wk_d = nc.dram_tensor("wk", [FS, DM], F32, kind="ExternalInput").ap()
    wv_d = nc.dram_tensor("wv", [FS, DM], F32, kind="ExternalInput").ap()
    bq_d = nc.dram_tensor("bq", [FS], F32, kind="ExternalInput").ap()
    bk_d = nc.dram_tensor("bk", [FS], F32, kind="ExternalInput").ap()
    bv_d = nc.dram_tensor("bv", [FS], F32, kind="ExternalInput").ap()
    bwT_d = nc.dram_tensor("bwT", [3, HPC], F32, kind="ExternalInput").ap()
    qw_d = nc.dram_tensor("qw", [DM // 8, FS], I32, kind="ExternalInput").ap()
    scales_d = nc.dram_tensor("scales", [FS], F32, kind="ExternalInput").ap()
    zeros_d = nc.dram_tensor("zeros", [FS], F32, kind="ExternalInput").ap()
    fbias_d = nc.dram_tensor("fbias", [FS], F32, kind="ExternalInput").ap()
    bivkt_d = nc.dram_tensor("bivkt", [3, 64], F32, kind="ExternalInput").ap()
    rev64_d = nc.dram_tensor("rev64", [64, 1], F32, kind="ExternalInput").ap()
    d64_d = nc.dram_tensor("d64", [64, 1], F32, kind="ExternalInput").ap()
    idm_d = nc.dram_tensor("idm", [128, 128], BF16, kind="ExternalInput").ap()
    shifts_d = nc.dram_tensor("shifts", [128, 1], I32, kind="ExternalInput").ap()

    out_d = nc.dram_tensor("out", [B, L, FS], F32, kind="ExternalOutput").ap()
    if debug:
        dbg = {}
        for nm, shp, dt in [
            ("dbg_t64", [64, HPC], F32), ("dbg_bdt", [128, 128], F32),
            ("dbg_wt", [128, 1024], BF16), ("dbg_bprj", [128, 4], F32),
            ("dbg_wf", [128, 1024], BF16), ("dbg_biasb", [128, 8], F32),
            ("dbg_q", [2, 128, L], BF16), ("dbg_k", [2, 128, L], BF16),
            ("dbg_v", [2, 128, L], BF16), ("dbg_g", [256, FS], F32),
            ("dbg_log", [128, L], F32), ("dbg_exp", [128, L], BF16),
            ("dbg_nm", [128, 1], F32), ("dbg_dn", [128, 1], F32),
            ("dbg_sf", [128, HD], F32), ("dbg_vt", [128, 32 * 257], BF16),
            ("dbg_at", [128, 32 * 128], BF16), ("dbg_lnst", [128, 8], F32),
            ("dbg_stats", [128, 48], F32), ("dbg_xtm", [128, DM], BF16),
            ("dbg_xnf", [128, 32 * T1], BF16),
        ]:
            dbg[nm] = nc.dram_tensor(nm, shp, dt, kind="ExternalOutput").ap()

    qS = nc.dram_tensor("qS", [HPC, B, 2, 128, L], BF16, kind="Internal").ap()
    kS = nc.dram_tensor("kS", [HPC, B, 2, 128, L], BF16, kind="Internal").ap()
    vS = nc.dram_tensor("vS", [HPC, B, 2, 128, L], BF16, kind="Internal").ap()
    gS = nc.dram_tensor("gS", [B, L, FS], F32, kind="Internal").ap()
    wfS = nc.dram_tensor("wfS", [128, 32 * FS], BF16, kind="Internal").ap()

    with ExitStack() as top:
        sem = lambda name: Sem(top.enter_context(nc.semaphore(name)), name)
        sbuf = lambda scope, name, shape, dt: scope.enter_context(
            nc.sbuf_tensor(name, shape, dt))
        psum = lambda scope, name, shape: scope.enter_context(
            nc.psum_tensor(name, shape, F32))

        idm = sbuf(top, "idm_sb", [128, 128], BF16)
        carriers = (sbuf(top, "carry_v", [1, 2], F32),
                    sbuf(top, "carry_s", [1, 2], F32),
                    sbuf(top, "carry_rb", [128, 1], F32))

        sA = ExitStack()
        sA.__enter__()
        WT = [sbuf(sA, f"WT{w}", [128, 32 * FS], BF16) for w in range(3)]
        fb_bc = sbuf(sA, "fb_bc", [128, FS], BF16)
        zro = sbuf(sA, "zro", [128, 128], BF16)
        bprj = [sbuf(sA, f"bprj{w}", [128, 4], F32) for w in range(3)]
        epsT = sbuf(sA, "epsT", [128, 1], F32)

        s0 = sem("s0")
        s_c = sem("s_c")
        s_bc = sem("s_bc")
        s_wl = sem("s_wl")
        s_wu = sem("s_wu")
        s_wn = sem("s_wn")
        s_tx0 = sem("s_tx0")
        s_txe = sem("s_txe")
        s_qwl = sem("s_qwl")
        s_qwd = sem("s_qwd")
        s_wfs = sem("s_wfs")

        s_x = sem("s_x")
        s_mv = sem("s_mv")
        s_std = sem("s_std")
        s_ap = sem("s_ap")
        s_trd = sem("s_trd")
        s_prj = sem("s_prj")
        s_pje = sem("s_pje")
        s_ffn = sem("s_ffn")
        s_ffe = sem("s_ffe")
        s_ftm = sem("s_ftm")
        s_sil = sem("s_sil")
        s_scq = sem("s_scq")
        s_scg = sem("s_scg")
        s_wf = sem("s_wf")
        s_xnr = sem("s_xnr")
        s_pjd = sem("s_pjd")

        s_pl = sem("s_pl")
        s_gaf = sem("s_gaf")
        s_gae = sem("s_gae")
        s_gbf = sem("s_gbf")
        s_gbe = sem("s_gbe")
        s_max = sem("s_max")
        s_exp = sem("s_exp")
        s_cpd = sem("s_cpd")
        s_dnm = sem("s_dnm")
        s_pmx = sem("s_pmx")
        s_tpf = sem("s_tpf")
        s_tpe = sem("s_tpe")
        s_vtr = sem("s_vtr")
        s_atr = sem("s_atr")
        s_avd = sem("s_avd")
        s_rc = sem("s_rc")
        s_sn = sem("s_sn")
        s_out = sem("s_out")
        s_gx = sem("s_gx")
        s_ost = sem("s_ost")

        with ExitStack() as sP0:
            bw_sb = sbuf(sP0, "bw_sb", [3, HPC], F32)
            bivkt_sb = sbuf(sP0, "bivkt_sb", [3, 64], F32)
            rev64_sb = sbuf(sP0, "rev64_sb", [64, 1], F32)
            d64_sb = sbuf(sP0, "d64_sb", [64, 1], F32)
            shifts_sb = sbuf(sP0, "shifts_sb", [128, 1], I32)
            t64 = sbuf(sP0, "t64", [64, HPC], F32)
            ones_row = sbuf(sP0, "ones_row", [1, 128], F32)
            srow = sbuf(sP0, "srow", [1, FS], F32)
            zrow = sbuf(sP0, "zrow", [1, FS], F32)
            frow = sbuf(sP0, "frow", [1, FS], F32)
            scales_b = sbuf(sP0, "scales_b", [128, FS], F32)
            zeros_b = sbuf(sP0, "zeros_b", [128, FS], F32)
            BDT = [sbuf(sP0, f"BDT{h}", [128, 128], F32) for h in range(HPC)]
            wtmp = [sbuf(sP0, f"wtmp{i}", [128, DM], F32) for i in range(2)]
            wnat = sbuf(sP0, "wnat", [128, DM], BF16)
            qwt = [sbuf(sP0, f"qwt{i}", [128, FS], I32) for i in range(2)]
            qwi = sbuf(sP0, "qwi", [128, FS], I32)
            qff = sbuf(sP0, "qff", [128, FS], F32)
            wfstg = [sbuf(sP0, f"wfstg{i}", [128, FS], BF16) for i in range(2)]
            ps_a = psum(sP0, "ps_a", [128, 512])
            ps_b = psum(sP0, "ps_b", [128, 512])
            ps_t = [psum(sP0, f"ps_t{i}", [128, 512]) for i in range(2)]

            p = Phase("P0a")
            n0 = 0
            loads = [
                (bw_sb[:, :], bwT_d[:, :]),
                (bivkt_sb[:, :], bivkt_d[:, :]),
                (rev64_sb[:, :], rev64_d[:, :]),
                (d64_sb[:, :], d64_d[:, :]),
                (idm[:, :], idm_d[:, :]),
                (shifts_sb[:, :], shifts_d[:, :]),
                (srow[:, :], scales_d.unsqueeze(0)),
                (zrow[:, :], zeros_d.unsqueeze(0)),
                (frow[:, :], fbias_d.unsqueeze(0)),
            ]
            for w, bd in ((0, bq_d), (1, bk_d), (2, bv_d)):
                for fc in range(4):
                    loads.append((bprj[w][:, fc:fc + 1],
                                  bd[fc * 128:(fc + 1) * 128].unsqueeze(1)))
            for dst, src in loads:
                p.add("sync", lambda e, d=dst, s=src: e.dma_start(d, s),
                      incs=[(s0, 16)])
                n0 += 1
            p.add("vector", lambda e: e.memset(ones_row[:, :], 1.0))
            p.add("vector", lambda e: e.memset(zro[:, :], 0.0))
            p.add("vector", lambda e: e.memset(epsT[:, :], EPS))
            p.add("vector", lambda e: e.memset(BDT[0][:, :], 0.0))
            p.add("vector", lambda e: e.memset(BDT[1][:, :], 0.0))
            p.add("sync", lambda e: e.nop(), waits=[(s0, 16 * n0)])
            p.run(nc, carriers)

            p = Phase("P0b")
            p.add("tensor",
                  lambda e: e.matmul(ps_a[0:64, 0:HPC], bivkt_sb[:, :],
                                     bw_sb[:, :], start=True, stop=True),
                  incs=[(s_c, 1)])
            p.add("vector",
                  lambda e: e.tensor_scalar(t64[:, :], ps_a[0:64, 0:HPC],
                                            rev64_sb[:, :], d64_sb[:, :],
                                            ALU.mult, ALU.add),
                  waits=[(s_c, 1)], cincs=[(s_c, 1)])
            for row, dst in ((srow, scales_b), (zrow, zeros_b), (frow, fb_bc)):
                p.add("tensor",
                      lambda e, r=row: e.matmul(ps_b[:, :], ones_row[:, :],
                                                r[:, :], start=True, stop=True),
                      waits=[(s_bc, s_bc.n)], incs=[(s_bc, 1)])
                p.add("vector",
                      lambda e, d=dst: e.tensor_copy(d[:, :], ps_b[:, :]),
                      waits=[(s_bc, s_bc.n)], incs=[(s_bc, 1)])
            p.run(nc, carriers)

            p = Phase("P0c")
            for h in range(HPC):
                for d in range(16):
                    p.add("sync",
                          lambda e, h=h, d=d: e.dma_start(
                              BDT[h][8 * d:8 * d + 8, 8 * d:8 * d + 8],
                              t64[:, h:h + 1]),
                          waits=[(s_c, 2)], incs=[(s0, 16)])
            p.add("sync", lambda e: e.nop(), waits=[(s0, s0.n)])
            if debug:
                p.add("gpsimd", lambda e: e.dma_start(dbg["dbg_t64"], t64[:, :]),
                      waits=[(s_c, 2)], incs=[(s0, 16)])
                p.add("gpsimd", lambda e: e.dma_start(dbg["dbg_bdt"], BDT[0][:, :]),
                      waits=[(s0, s0.n - 16)], incs=[(s0, 16)])
                p.add("gpsimd", lambda e: e.nop(), waits=[(s0, s0.n)])
            p.run(nc, carriers)

            p = Phase("P0d")
            wsrc = (wq_d, wk_d, wv_d)
            fcg = 0

            def tx_group(get_lhs, evac_fn, extra_waits=()):
                bank = s_tx0.n % 2
                for q in range(4):
                    w0 = list(extra_waits) if q == 0 else []
                    if q == 0:
                        w0.append((s_txe, max(0, s_tx0.n - 1)))
                    p.add("tensor",
                          lambda e, bank=bank, q=q, lhs=get_lhs(q): e.matmul(
                              ps_t[bank][:, q * 128:(q + 1) * 128], lhs,
                              idm[:, :], start=True, stop=True),
                          waits=w0, incs=[(s_tx0, 1)] if q == 3 else ())
                p.add("vector", lambda e, bank=bank: evac_fn(e, bank),
                      waits=[(s_tx0, s_tx0.n)], incs=[(s_txe, 1)])

            wn_after = []
            for w in range(3):
                for fc in range(4):
                    par = fcg % 2
                    p.add("sync",
                          lambda e, w=w, fc=fc, par=par: e.dma_start(
                              wtmp[par][:, :],
                              wsrc[w][fc * 128:(fc + 1) * 128, :]),
                          waits=([(s_wn, wn_after[fcg - 2])]
                                 if fcg >= 2 else []),
                          incs=[(s_wl, 16)])
                    if w == 0:
                        h = fc // 2
                        for seg in range(8):
                            pp = ps_a if seg % 2 == 0 else ps_b
                            w0 = [(s_wl, 16 * (fcg + 1))]
                            if seg == 0:
                                w0.append((s_wn, s_wn.n))
                            else:
                                w0.append((s_wn, max(0, s_wn.n - 1)))
                            p.add("tensor",
                                  lambda e, h=h, par=par, seg=seg, pp=pp:
                                  e.matmul(pp[:, :], BDT[h][:, :],
                                           wtmp[par][:, seg * 512:
                                                     (seg + 1) * 512],
                                           start=True, stop=True),
                                  waits=w0, incs=[(s_c, 1)])
                            wv = [(s_c, s_c.n)]
                            if seg == 0:
                                wv.append((s_tx0, s_tx0.n))
                            p.add("vector",
                                  lambda e, seg=seg, pp=pp: e.tensor_copy(
                                      wnat[:, seg * 512:(seg + 1) * 512],
                                      pp[:, :]),
                                  waits=wv, cincs=[(s_wn, 1)])
                        wn_ready = s_wn.n
                        wn_after.append(wn_ready)
                        p.add("tensor",
                              lambda e, h=h, fc=fc: e.matmul(
                                  ps_b[:, 0:1], BDT[h][:, :],
                                  bprj[0][:, fc:fc + 1], start=True, stop=True),
                              waits=[(s_wn, s_wn.n)], incs=[(s_c, 1)])
                        p.add("vector",
                              lambda e, fc=fc: e.tensor_copy(
                                  bprj[0][:, fc:fc + 1], ps_b[:, 0:1]),
                              waits=[(s_c, s_c.n)], cincs=[(s_wn, 1)])
                    else:
                        p.add("vector",
                              lambda e, par=par: e.tensor_copy(wnat[:, :],
                                                               wtmp[par][:, :]),
                              waits=[(s_wl, 16 * (fcg + 1)),
                                     (s_tx0, s_tx0.n)],
                              cincs=[(s_wn, 1)])
                        wn_ready = s_wn.n
                        wn_after.append(wn_ready)
                    for g in range(8):
                        def evac(e, bank, w=w, fc=fc, g=g):
                            dst = WT[w][:, :].rearrange(
                                "p (cc f) -> p cc f", f=512)[
                                :, g * 4:(g + 1) * 4, fc * 128:(fc + 1) * 128]
                            src = ps_t[bank][:, :].rearrange(
                                "p (q f) -> p q f", f=128)
                            return e.tensor_copy(dst, src)

                        def get_lhs(q, g=g):
                            cc = g * 4 + q
                            return wnat[:, cc * 128:(cc + 1) * 128]

                        tx_group(get_lhs, evac,
                                 extra_waits=[(s_wn, wn_ready)] if g == 0
                                 else ())
                    fcg += 1
            for cc in range(32):
                par = cc % 2
                for e8 in range(8):
                    p.add("sync",
                          lambda e, cc=cc, e8=e8, par=par: e.dma_start(
                              qwt[par][e8::8, :],
                              qw_d[16 * cc:16 * cc + 16, :]),
                          waits=([(s_qwd, max(0, cc - 1))] if e8 == 0 else []),
                          incs=[(s_qwl, 16)])
                p.add("vector",
                      lambda e, par=par: e.tensor_tensor(
                          qwi[:, :], qwt[par][:, :],
                          shifts_sb[:, :].broadcast_to([128, FS]),
                          ALU.logical_shift_right),
                      waits=[(s_qwl, 16 * 8 * (cc + 1))])
                p.add("vector",
                      lambda e: e.tensor_single_scalar(qwi[:, :], qwi[:, :],
                                                       15, ALU.bitwise_and))
                p.add("vector", lambda e: e.tensor_copy(qff[:, :], qwi[:, :]))
                p.add("vector",
                      lambda e: e.tensor_mul(qff[:, :], qff[:, :],
                                             scales_b[:, :]))
                p.add("vector",
                      lambda e, par=par: e.tensor_sub(wfstg[par][:, :],
                                                      qff[:, :],
                                                      zeros_b[:, :]),
                      waits=[(s_wfs, 16 * max(0, cc - 1))],
                      cincs=[(s_qwd, 1)])
                p.add("gpsimd",
                      lambda e, cc=cc, par=par: e.dma_start(
                          wfS[:, cc * FS:(cc + 1) * FS], wfstg[par][:, :]),
                      waits=[(s_qwd, s_qwd.n)], incs=[(s_wfs, 16)])
            p.add("gpsimd", lambda e: e.nop(), waits=[(s_wfs, s_wfs.n)])
            p.add("sync", lambda e: e.nop(),
                  waits=[(s_qwl, s_qwl.n), (s_wl, s_wl.n)])
            if debug:
                p.add("gpsimd", lambda e: e.dma_start(dbg["dbg_wt"],
                      WT[0][:, 0:1024]), incs=[(s0, 16)])
                p.add("gpsimd", lambda e: e.dma_start(dbg["dbg_bprj"],
                      bprj[0][:, :]), incs=[(s0, 16)])
                p.add("gpsimd", lambda e: e.dma_start(dbg["dbg_wf"],
                      wfS[:, 0:1024]), incs=[(s0, 16)])
                p.add("gpsimd", lambda e: e.dma_start(dbg["dbg_biasb"],
                      fb_bc[:, 0:8]), incs=[(s0, 16)])
                p.add("gpsimd", lambda e: e.nop(), waits=[(s0, s0.n)])
            p.run(nc, carriers)

        if nphases < 2:
            sA.__exit__(None, None, None)
            return nc
        with ExitStack() as sP1:
            xbuf = [sbuf(sP1, f"xbuf{i}", [128, DM], F32) for i in range(2)]
            xn_tm = sbuf(sP1, "xn_tm", [128, DM], BF16)
            xn_fm = [sbuf(sP1, f"xn_fm{i}", [128, 32 * T1], BF16)
                     for i in range(2)]
            wfbuf = sbuf(sP1, "wfbuf", [128, 32 * FS], BF16)
            stats = sbuf(sP1, "stats", [128, 48], F32)
            lnst = sbuf(sP1, "lnst", [128, 8], F32)
            lnsta = sbuf(sP1, "lnsta", [128, 2], F32)
            mvb = [lnst[:, 0:2], lnst[:, 2:4]]
            stdb = [lnsta[:, 0:1], lnsta[:, 1:2]]
            rstdb = [lnst[:, 4:5], lnst[:, 5:6]]
            betab = [lnst[:, 6:7], lnst[:, 7:8]]
            qstg = [sbuf(sP1, f"qstg{i}", [128, T1], BF16) for i in range(2)]
            gstg = [sbuf(sP1, f"gstg{i}", [128, FS], F32) for i in range(2)]
            ps_tx = [psum(sP1, f"ps_tx{i}", [128, 512]) for i in range(2)]
            ps_pj = [psum(sP1, f"ps_pj{i}", [128, 512]) for i in range(4)]
            ps_f = [psum(sP1, f"ps_f{i}", [128, 512]) for i in range(2)]

            SCR = (qS, kS, vS)
            p = Phase("P1")
            txe_base = s_txe.n

            def emit_xload(t):
                b, row0 = t // 32, (t % 32) * 128
                p.add("sync",
                      lambda e, b=b, row0=row0, par=t % 2: e.dma_start(
                          xbuf[par][:, :], x_d[b, row0:row0 + 128, :]),
                      waits=[(s_ap, max(0, t - 1))], incs=[(s_x, 16)])

            def emit_prep(t):
                par = t % 2
                for sseg in range(8):
                    p.add("vector",
                          lambda e, par=par, sseg=sseg: e.bn_stats(
                              stats[:, 6 * sseg:6 * (sseg + 1)],
                              xbuf[par][:, 512 * sseg:512 * (sseg + 1)]),
                          waits=[(s_x, 16 * (t + 1))] if sseg == 0 else [])
                p.add("vector",
                      lambda e, par=par: e.bn_aggr(mvb[par],
                                                   stats[:, :]),
                      waits=[(s_std, max(0, t - 1))], cincs=[(s_mv, 1)])
                p.add("scalar",
                      lambda e, par=par: e.activation(
                          stdb[par], lnst[:, 2*par+1:2*par+2], AF.Sqrt,
                          bias=epsT[:, :], scale=1.0),
                      waits=[(s_mv, t + 1), (s_ap, max(0, t - 1))],
                      cincs=[(s_std, 1)])
                p.add("vector",
                      lambda e, par=par: e.reciprocal(rstdb[par],
                                                      stdb[par]),
                      waits=[(s_std, t + 1)])

            def emit_prepB(t):
                par = t % 2
                p.add("vector",
                      lambda e, par=par: e.tensor_tensor(
                          xbuf[par][:, :], xbuf[par][:, :],
                          lnst[:, 2 * par:2 * par + 1].broadcast_to(
                              [128, DM]), ALU.subtract))
                p.add("vector",
                      lambda e, par=par: e.tensor_tensor(
                          xn_tm[:, :], xbuf[par][:, :],
                          rstdb[par].broadcast_to([128, DM]), ALU.mult),
                      waits=[(s_txe, max(0, txe_base + 8 * t))],
                      cincs=[(s_ap, 1)])
                cpar = (t // 2) % 2
                tl = t % 2
                for g in range(8):
                    bank = s_tx0.n % 2
                    for q in range(4):
                        cc = g * 4 + q
                        w0 = []
                        if q == 0:
                            w0 = [(s_txe, max(0, s_tx0.n - 1))]
                            if g == 0:
                                w0.append((s_ap, s_ap.n))
                                if t >= 4:
                                    c_prev = t // 2 - 1
                                    w0.append((s_pje, 12 * c_prev))
                                    w0.append((s_ffe, 2 * c_prev))
                        p.add("tensor",
                              lambda e, bank=bank, q=q, cc=cc: e.matmul(
                                  ps_tx[bank][:, q * 128:(q + 1) * 128],
                                  xn_tm[:, cc * 128:(cc + 1) * 128],
                                  idm[:, :], start=True, stop=True),
                              waits=w0)
                    p.add("tensor",
                          lambda e, bank=bank: e.matmul(
                              ps_tx[bank][:, 3 * 128:4 * 128],
                              idm[:, :], zro[:, :],
                              start=False, stop=True),
                          incs=[(s_tx0, 1)])

                    last_of_chunk = (tl == 1 and g == 7)
                    for q in range(4):
                        cc = g * 4 + q
                        last = (q == 3)

                        def evac(e, cpar=cpar, tl=tl, cc=cc, q=q, bank=bank):
                            dst = xn_fm[cpar][:, cc * T1 + tl * 128:
                                              cc * T1 + (tl + 1) * 128]
                            return e.tensor_copy(
                                dst, ps_tx[bank][:, q * 128:(q + 1) * 128])
                        p.add("vector", evac,
                              waits=[(s_tx0, s_tx0.n)] if q == 0 else [],
                              incs=[(s_txe, 1)] if last else [],
                              cincs=[(s_xnr, 1)]
                              if (last and last_of_chunk) else ())

            def emit_wfload(c):
                for sl in range(4):
                    p.add("sync",
                          lambda e, sl=sl: e.dma_start(
                              wfbuf[:, sl * 8 * FS:(sl + 1) * 8 * FS],
                              wfS[:, sl * 8 * FS:(sl + 1) * 8 * FS]),
                          waits=[(s_ffe, 2 * c)] if sl == 0 else [],
                          incs=[(s_wf, 16)])

            def emit_mm(c):
                b, tc = c // CPB, c % CPB
                t0 = tc * T1
                cpar = c % 2
                xn_ready = txe_base + 16 * (c + 1)
                for w in range(3):
                    for fc in range(4):
                        gidx = c * 12 + w * 4 + fc
                        bank = gidx % 4
                        for cc in range(32):
                            w0 = []
                            if cc == 0:
                                w0 = [(s_xnr, c + 1),
                                      (s_pje, max(0, gidx - 3))]
                            p.add("tensor",
                                  lambda e, w=w, fc=fc, cc=cc, bank=bank,
                                         cpar=cpar: e.matmul(
                                      ps_pj[bank][:, 0:T1],
                                      WT[w][:, cc * 512 + fc * 128:
                                            cc * 512 + (fc + 1) * 128],
                                      xn_fm[cpar][:, cc * T1:(cc + 1) * T1],
                                      start=(cc == 0), stop=(cc == 31)),
                                  waits=w0,
                                  incs=[(s_prj, 1)] if cc == 31 else ())
                        stg = gidx % 2
                        p.add("scalar",
                              lambda e, w=w, fc=fc, bank=bank, stg=stg:
                              e.activation(qstg[stg][:, :],
                                           ps_pj[bank][:, 0:T1], AF.Identity,
                                           bias=bprj[w][:, fc:fc + 1],
                                           scale=1.0),
                              waits=[(s_prj, gidx + 1),
                                     (s_scq, 16 * max(0, gidx - 1))],
                              incs=[(s_pje, 1)], cincs=[(s_pjd, 1)],
                              rb=qstg[stg][:, T1 - 1:T1])
                        p.add("gpsimd",
                              lambda e, w=w, fc=fc, b=b, t0=t0, stg=stg:
                              e.dma_start(
                                  SCR[w][fc // 2, b, fc % 2, :, t0:t0 + T1],
                                  qstg[stg][:, :]),
                              waits=[(s_pjd, gidx + 1)], incs=[(s_scq, 16)])
                for ts in range(2):
                    p.add("tensor",
                          lambda e, ts=ts: e.matmul(
                              ps_f[ts][:, :], idm[:, :], fb_bc[:, :],
                              start=True, stop=False),
                          waits=[(s_xnr, c + 1),
                                 (s_ffe, max(0, 2 * (c - 1) + ts + 1)),
                                 (s_wf, 64 * (c + 1))])
                    for cc in range(32):
                        p.add("tensor",
                              lambda e, cc=cc, ts=ts, cpar=cpar: e.matmul(
                                  ps_f[ts][:, :],
                                  xn_fm[cpar][:, cc * T1 + ts * 128:
                                              cc * T1 + (ts + 1) * 128],
                                  wfbuf[:, cc * FS:(cc + 1) * FS],
                                  start=False, stop=(cc == 31)),
                              incs=[(s_ffn, 1)] if cc == 31 else ())
                for ts in range(2):
                    fseq = 2 * c + ts
                    p.add("scalar",
                          lambda e, ts=ts: e.activation(gstg[ts][:, :],
                                                        ps_f[ts][:, :],
                                                        AF.Silu),
                          waits=[(s_ffn, fseq + 1),
                                 (s_scg, 16 * max(0, fseq - 1))],
                          cincs=[(s_sil, 1), (s_ffe, 1)],
                          rb=gstg[ts][:, FS - 1:FS])
                    p.add("gpsimd",
                          lambda e, ts=ts, b=b, t0=t0: e.dma_start(
                              gS[b, t0 + ts * 128:t0 + (ts + 1) * 128, :],
                              gstg[ts][:, :]),
                          waits=[(s_sil, fseq + 1)], incs=[(s_scg, 16)])

            nt1 = NT1 if p1_cap is None else p1_cap
            for c in range(nt1 + 2):
                if c < nt1:
                    emit_xload(2 * c)
                    emit_xload(2 * c + 1)
                if 1 <= c <= nt1:
                    emit_wfload(c - 1)
                    emit_prep(2 * (c - 1))
                    emit_prep(2 * (c - 1) + 1)
                    emit_prepB(2 * (c - 1))
                    emit_prepB(2 * (c - 1) + 1)
                if c >= 2:
                    emit_mm(c - 2)
            p.add("gpsimd", lambda e: e.nop(),
                  waits=[(s_scq, s_scq.n), (s_scg, s_scg.n)])
            p.add("sync", lambda e: e.nop(),
                  waits=[(s_x, s_x.n), (s_wf, s_wf.n)])
            if debug:
                for nm, S in (("dbg_q", qS), ("dbg_k", kS), ("dbg_v", vS)):
                    p.add("gpsimd", lambda e, nm=nm, S=S: e.dma_start(
                        dbg[nm], S[0, 0, :, :, :]), incs=[(s0, 16)])
                p.add("gpsimd", lambda e: e.dma_start(
                    dbg["dbg_g"], gS[0, 0:256, :]), incs=[(s0, 16)])
                p.add("gpsimd", lambda e: e.dma_start(
                    dbg["dbg_lnst"], lnst[:, 0:8]), incs=[(s0, 16)])
                p.add("gpsimd", lambda e: e.dma_start(
                    dbg["dbg_stats"], stats[:, :]), incs=[(s0, 16)])
                p.add("gpsimd", lambda e: e.dma_start(
                    dbg["dbg_xtm"], xn_tm[:, :]), incs=[(s0, 16)])
                p.add("gpsimd", lambda e: e.dma_start(
                    dbg["dbg_xnf"], xn_fm[0][:, :]), incs=[(s0, 16)])
                p.add("gpsimd", lambda e: e.nop(), waits=[(s0, s0.n)])
            p.run(nc, carriers)
        sA.__exit__(None, None, None)

        if nphases < 3:
            return nc
        with ExitStack() as sP2:
            qb = [sbuf(sP2, f"qb{i}", [128, 2 * L], BF16) for i in range(2)]
            kb = [sbuf(sP2, f"kb{i}", [128, 2 * L], BF16) for i in range(2)]
            vb = [sbuf(sP2, f"vb{i}", [128, 2 * L], BF16) for i in range(2)]
            vT = sbuf(sP2, "vT", [128, 32 * 257], BF16)
            logSB = [sbuf(sP2, f"logSB{i}", [128, L], F32) for i in range(2)]
            expSB = [sbuf(sP2, f"expSB{i}", [128, L], BF16) for i in range(2)]
            attnT = [sbuf(sP2, f"attnT{i}", [128, 32 * 128], BF16)
                     for i in range(2)]
            negmx = [sbuf(sP2, f"negmx{i}", [128, 1], F32) for i in range(2)]
            mx4 = sbuf(sP2, "mx4", [128, 4], F32)
            zcol = sbuf(sP2, "zcol", [128, 1], F32)
            denom = [sbuf(sP2, f"denom{i}", [128, 1], F32) for i in range(2)]
            recip = [sbuf(sP2, f"recip{i}", [128, 1], F32) for i in range(2)]
            sfb = [sbuf(sP2, f"sfb{i}", [128, HD], F32) for i in range(2)]
            sgb = [sbuf(sP2, f"sgb{i}", [128, HD], F32) for i in range(2)]
            gt = [sbuf(sP2, f"gt{i}", [128, HD], F32) for i in range(4)]
            xt = [sbuf(sP2, f"xt{i}", [128, HD], F32) for i in range(4)]
            ot = [sbuf(sP2, f"ot{i}", [128, HD], F32) for i in range(4)]
            ps_ga = psum(sP2, "ps_ga", [128, 1024])
            ps_gb = psum(sP2, "ps_gb", [128, 1024])
            ps_tp = [psum(sP2, f"ps_tp{i}", [128, 512]) for i in range(2)]
            ps_av = [psum(sP2, f"ps_av{i}", [128, 512]) for i in range(2)]

            pairs = [(b, h) for b in range(B) for h in range(HPC)][:npair]
            nlc = npair * LCH if nlc_cap is None else nlc_cap
            p = Phase("P2")
            tpe_after = []

            def emit_pair_load(j):
                b, h = pairs[j]
                par = j % 2
                for djc in range(2):
                    for buf, S in ((qb, qS), (kb, kS), (vb, vS)):
                        w0 = []
                        if j >= 2:
                            w0 = [(s_gae, 64 * (j - 1)),
                                  (s_vtr, 17 * (j - 1))]
                        p.add("sync",
                              lambda e, buf=buf, S=S, par=par, h=h, b=b,
                                     djc=djc: e.dma_start(
                                  buf[par][:, djc * L:(djc + 1) * L],
                                  S[h, b, djc, :, :]),
                              waits=w0, incs=[(s_pl, 16)])

            def emit_vtr(j):
                for vt in range(16):
                    bank = s_tpf.n % 2
                    for q in range(4):
                        col = vt * 512 + q * 128
                        mc, djc = col // 256, (col % 256) // 128
                        w0 = []
                        if q == 0:
                            w0 = [(s_tpe, max(0, s_tpf.n - 1)),
                                  (s_pl, 96 * (j + 1))]
                        p.add("tensor",
                              lambda e, bank=bank, q=q, mc=mc, djc=djc,
                                     par=j % 2: e.matmul(
                                  ps_tp[bank][:, q * 128:(q + 1) * 128],
                                  vb[par][:, djc * L + mc * 128:
                                          djc * L + (mc + 1) * 128],
                                  idm[:, :], start=True, stop=True),
                              waits=w0, incs=[(s_tpf, 1)] if q == 3 else ())
                    w1 = [(s_tpf, s_tpf.n)]
                    if vt == 0 and j >= 1:
                        w1.append((s_avd, 32 * j))

                    def vevac(e, vt=vt, bank=bank):
                        dst = vT[:, :].rearrange("p (mc c) -> p mc c", c=257)[
                            :, 2 * vt:2 * vt + 2, 0:256]
                        srcv = ps_tp[bank][:, :].rearrange(
                            "p (t c) -> p t c", c=256)
                        return e.tensor_copy(dst, srcv)
                    p.add("vector", vevac,
                          waits=w1, incs=[(s_tpe, 1)], cincs=[(s_vtr, 1)])
                p.add("vector",
                      lambda e: e.memset(
                          vT[:, :].rearrange("p (mc c) -> p mc c", c=257)[
                              :, :, 256:257], 1.0),
                      cincs=[(s_vtr, 1)])

            def emit_gxload(gi):
                j, li = gi // LCH, gi % LCH
                b, h = pairs[j]
                l0 = li * 128
                p.add("sync",
                      lambda e, b=b, h=h, l0=l0, r=gi % 4: e.dma_start(
                          gt[r][:, :],
                          gS[b, l0:l0 + 128, h * HD:(h + 1) * HD]),
                      waits=[(s_out, max(0, gi - 3))], incs=[(s_gx, 16)])
                p.add("sync",
                      lambda e, b=b, h=h, l0=l0, r=gi % 4: e.dma_start(
                          xt[r][:, :],
                          xres_d[b, l0:l0 + 128, h * HD:(h + 1) * HD]),
                      incs=[(s_gx, 16)])

            def emit_SA(gi):
                j, li = gi // LCH, gi % LCH
                par = j % 2
                gpar = gi % 2
                l0 = li * 128
                for rep in range(2):
                    for half, (ps_g, sf, se) in enumerate(
                            ((ps_ga, s_gaf, s_gae), (ps_gb, s_gbf, s_gbe))):
                        for sub in range(2):
                            mts = rep * 4 + half * 2 + sub
                            m0 = mts * 512
                            for djc in range(2):
                                w0 = []
                                if sub == 0 and djc == 0:
                                    w0 = [(se, sf.n)]
                                    if li == 0:
                                        w0.append((s_pl, 96 * (j + 1)))
                                inc = []
                                if sub == 1 and djc == 1:
                                    inc = [(sf, 1)]
                                p.add("tensor",
                                      lambda e, ps_g=ps_g, sub=sub, djc=djc,
                                             par=par, l0=l0, m0=m0: e.matmul(
                                          ps_g[:, sub * 512:(sub + 1) * 512],
                                          qb[par][:, djc * L + l0:
                                                  djc * L + l0 + 128],
                                          kb[par][:, djc * L + m0:
                                                  djc * L + m0 + 512],
                                          start=(djc == 0), stop=(djc == 1)),
                                      waits=w0, incs=inc)
                        seg = rep * 2 + half
                        w1 = [(sf, sf.n)]
                        if gi >= 2 and seg == 0:
                            w1.append((s_max, gi - 1))
                        p.add("vector",
                              lambda e, ps_g=ps_g, seg=seg: e.reduce_max(
                                  mx4[:, seg:seg + 1], ps_g[:, :], axis=AX.X),
                              waits=w1)
                        p.add("vector",
                              lambda e, ps_g=ps_g, gpar=gpar, seg=seg:
                              e.tensor_copy(logSB[gpar][:, seg * 1024:
                                                        (seg + 1) * 1024],
                                            ps_g[:, :]),
                              incs=[(se, 1)])

                p.add("vector",
                      lambda e, gpar=gpar: e.reduce_max(
                          negmx[gpar][:, :], mx4[:, 0:4], axis=AX.X,
                          negate=True),
                      waits=[(s_exp, max(0, gi - 1))],
                      incs=[], cincs=[(s_max, 1)])

            def emit_SB_tr(gi):
                gpar = gi % 2
                w0 = [(s_max, gi + 1)]
                if gi >= 2:
                    w0.append((s_tpe, tpe_after[gi - 2]))
                p.add("scalar",
                      lambda e, gpar=gpar: e.activation(
                          expSB[gpar][:, :], logSB[gpar][:, :], AF.Exp,
                          bias=negmx[gpar][:, :], scale=1.0),
                      waits=w0, cincs=[(s_exp, 1)])
                for r in range(8):
                    bank = s_tpf.n % 2
                    for q in range(4):
                        mc = r * 4 + q
                        w1 = []
                        if q == 0:
                            w1 = [(s_tpe, max(0, s_tpf.n - 1)),
                                  (s_exp, gi + 1)]
                        inc = [(s_tpf, 1)] if q == 3 else []
                        p.add("tensor",
                              lambda e, bank=bank, q=q, mc=mc, gpar=gpar:
                              e.matmul(
                                  ps_tp[bank][:, q * 128:(q + 1) * 128],
                                  expSB[gpar][:, mc * 128:(mc + 1) * 128],
                                  idm[:, :], start=True, stop=True),
                              waits=w1, incs=inc)
                    w2 = [(s_tpf, s_tpf.n)]
                    if gi >= 2:
                        w2.append((s_avd, gi - 1))
                    p.add("vector",
                          lambda e, r=r, bank=bank, gpar=gpar: e.tensor_copy(
                              attnT[gpar][:, r * 512:(r + 1) * 512],
                              ps_tp[bank][:, :]),
                          waits=w2, incs=[(s_tpe, 1)],
                          cincs=[(s_atr, 1)] if r == 7 else ())
                tpe_after.append(s_tpe.n)

            def emit_SB_av(gi):
                gpar = gi % 2
                j = gi // LCH
                for mc in range(32):
                    w3 = []
                    if mc == 0:
                        w3 = [(s_atr, gi + 1), (s_vtr, 17 * (j + 1)),
                              (s_sn, max(0, gi - 1))]
                    inc = []
                    if mc == 31:
                        inc = [(s_avd, 1)]
                    p.add("tensor",
                          lambda e, mc=mc, gpar=gpar: e.matmul(
                              ps_av[gpar][:, 0:HD + 1],
                              attnT[gpar][:, mc * 128:(mc + 1) * 128],
                              vT[:, mc * 257:(mc + 1) * 257],
                              start=(mc == 0), stop=(mc == 31)),
                          waits=w3, incs=inc)

            def emit_SC(gi):
                j, li = gi // LCH, gi % LCH
                b, h = pairs[j]
                gpar = gi % 2
                r4 = gi % 4
                l0 = li * 128
                if gi + 1 < nlc:
                    ngpar = (gi + 1) % 2
                    p.add("vector",
                          lambda e, ngpar=ngpar: e.reciprocal(
                              recip[ngpar][:, :], ps_av[ngpar][:, HD:HD + 1]),
                          waits=[(s_avd, gi + 2)])
                p.add("vector",
                      lambda e, gpar=gpar: e.tensor_tensor(
                          sfb[gpar][:, :], ps_av[gpar][:, 0:HD],
                          recip[gpar][:, :].broadcast_to([128, HD]),
                          ALU.mult),
                      waits=[(s_avd, gi + 1)],
                      incs=[(s_sn, 1)])
                p.add("vector",
                      lambda e, gpar=gpar, r4=r4: e.tensor_mul(
                          sgb[gpar][:, :], sfb[gpar][:, :], gt[r4][:, :]),
                      waits=[(s_gx, 32 * (gi + 1))])
                p.add("vector",
                      lambda e, gpar=gpar, r4=r4: e.tensor_add(
                          ot[r4][:, :], sgb[gpar][:, :], xt[r4][:, :]),
                      waits=[(s_ost, 16 * max(0, gi - 3))],
                      cincs=[(s_out, 1)])
                p.add("gpsimd",
                      lambda e, b=b, h=h, l0=l0, r4=r4: e.dma_start(
                          out_d[b, l0:l0 + 128, h * HD:(h + 1) * HD],
                          ot[r4][:, :]),
                      waits=[(s_out, gi + 1)], incs=[(s_ost, 16)])

            p.add("vector", lambda e: e.memset(zcol[:, :], 0.0))
            emit_pair_load(0)
            for gi in range(nlc + 2):
                j, li = gi // LCH, gi % LCH
                if 1 <= gi <= nlc:
                    emit_SB_tr(gi - 1)
                if gi < nlc:
                    if li == 16 and j + 1 < npair:
                        emit_pair_load(j + 1)
                    emit_gxload(gi)
                    emit_SA(gi)
                if 1 <= gi <= nlc:
                    emit_SB_av(gi - 1)
                if gi == 1:
                    p.add("vector",
                          lambda e: e.reciprocal(
                              recip[0][:, :], ps_av[0][:, HD:HD + 1]),
                          waits=[(s_avd, 1)])
                if gi < nlc and li == 0:
                    emit_vtr(j)
                if gi >= 2:
                    emit_SC(gi - 2)
            p.add("gpsimd", lambda e: e.nop(), waits=[(s_ost, s_ost.n)])
            p.add("sync", lambda e: e.nop(),
                  waits=[(s_pl, s_pl.n), (s_gx, s_gx.n)])
            if debug:
                for nm, src_ap in (("dbg_log", logSB[0][:, :]),
                                   ("dbg_exp", expSB[0][:, :]),
                                   ("dbg_nm", negmx[0][:, :]),
                                   ("dbg_dn", recip[0][:, :]),
                                   ("dbg_sf", sfb[0][:, :]),
                                   ("dbg_vt", vT[:, :]),
                                   ("dbg_at", attnT[0][:, :])):
                    p.add("gpsimd", lambda e, nm=nm, s_=src_ap: e.dma_start(
                        dbg[nm], s_), incs=[(s0, 16)])
                p.add("gpsimd", lambda e: e.nop(), waits=[(s0, s0.n)])
            p.run(nc, carriers)

    return nc


def kernel(L0_attn_x, L0_attn_Wq, L0_attn_bq, L0_attn_Wk, L0_attn_bk,
           L0_attn_Wv, L0_attn_bv, L0_attn_bias_weight, L0_ffn_qweight,
           L0_ffn_scales, L0_ffn_zeros, L0_ffn_bias):
    x = np.ascontiguousarray(np.asarray(L0_attn_x, dtype=np.float32))
    Wq = np.asarray(L0_attn_Wq, dtype=np.float32)
    Wk = np.asarray(L0_attn_Wk, dtype=np.float32)
    Wv = np.asarray(L0_attn_Wv, dtype=np.float32)
    bq = np.asarray(L0_attn_bq, dtype=np.float32)
    bk = np.asarray(L0_attn_bk, dtype=np.float32)
    bv = np.asarray(L0_attn_bv, dtype=np.float32)
    bw = np.asarray(L0_attn_bias_weight, dtype=np.float32)
    qw = np.asarray(L0_ffn_qweight, dtype=np.int32)
    sc = np.asarray(L0_ffn_scales, dtype=np.float32)
    zr = np.asarray(L0_ffn_zeros, dtype=np.float32)
    fb = np.asarray(L0_ffn_bias, dtype=np.float32)

    nc = build_program()
    in_maps = []
    for c in range(NCORES):
        f0, f1 = FS * c, FS * (c + 1)
        in_maps.append({
            "x": x,
            "xres": np.ascontiguousarray(x[:, :, f0:f1]),
            "wq": np.ascontiguousarray(Wq[f0:f1, :]),
            "wk": np.ascontiguousarray(Wk[f0:f1, :]),
            "wv": np.ascontiguousarray(Wv[f0:f1, :]),
            "bq": np.ascontiguousarray(bq[f0:f1]),
            "bk": np.ascontiguousarray(bk[f0:f1]),
            "bv": np.ascontiguousarray(bv[f0:f1]),
            "bwT": np.ascontiguousarray(bw[HPC * c:HPC * (c + 1), :].T),
            "qw": np.ascontiguousarray(qw[:, f0:f1]),
            "scales": np.ascontiguousarray(sc[f0:f1, 0]),
            "zeros": np.ascontiguousarray(zr[f0:f1, 0]),
            "fbias": np.ascontiguousarray(fb[f0:f1]),
            "bivkt": _BIVKT,
            "rev64": _REV64,
            "d64": _D64,
            "idm": _IDENT,
            "shifts": _SHIFTS,
        })
    res = run_bass_kernel_spmd(nc, in_maps, core_ids=list(range(NCORES)))
    return np.concatenate([np.asarray(r["out"]) for r in res.results], axis=2)


# version 13
# speedup vs baseline: 1.0596x; 1.0596x over previous
import numpy as np
import ml_dtypes
import concourse.bass as bass
import concourse.mybir as mybir
from concourse.bass_utils import run_bass_kernel_spmd
from contextlib import ExitStack

F32 = mybir.dt.float32
BF16 = mybir.dt.bfloat16
I32 = mybir.dt.int32
AF = mybir.ActivationFunctionType
ALU = mybir.AluOpType
AX = mybir.AxisListType

B, L, DM = 4, 4096, 4096
NH, CD, NB = 16, 32, 8
HD = CD * NB
NCORES = 8
HPC = NH // NCORES
FS = HPC * HD
T1 = 256
NT1 = (B * L) // T1
CPB = L // T1
NPAIR = B * HPC
LCH = L // 128
NLC = NPAIR * LCH
EPS = 1e-5

_ENGINES = ("sync", "gpsimd", "vector", "scalar", "tensor")


def _tables():
    g = [1.0, 1.0, 1.0]
    dim, n = 3, 8
    masks = sorted(range(n), key=lambda m: (bin(m).count("1"), m))
    inv = {m: i for i, m in enumerate(masks)}
    C = np.zeros((n, n, n), dtype=np.float64)
    for i, a in enumerate(masks):
        for j, b_ in enumerate(masks):
            sw, at = 0, a >> 1
            while at:
                sw += bin(at & b_).count("1")
                at >>= 1
            s = -1.0 if sw & 1 else 1.0
            for t in range(dim):
                if a >> t & 1 and b_ >> t & 1:
                    s *= g[t]
            C[i, j, inv[a ^ b_]] = s
    grades = np.array([bin(m).count("1") for m in masks])
    rev = np.where(grades * (grades - 1) // 2 % 2 == 1, -1.0, 1.0)
    met = np.array([np.prod([g[t] for t in range(dim) if m >> t & 1] + [1.0])
                    for m in masks])
    return C, rev, met


_C_TAB, _REV, _MET = _tables()
_BIV_IDX = [4, 5, 6]
_SCALE = 1.0 / np.sqrt(CD * NB)

_BIVKT = np.ascontiguousarray(
    _C_TAB[:, :, _BIV_IDX].transpose(2, 0, 1).reshape(3, 64)).astype(np.float32)
_REV64 = np.repeat(_REV, 8).reshape(64, 1).astype(np.float32)
_D64 = np.zeros((64, 1), dtype=np.float64)
for _i in range(8):
    _D64[_i * 8 + _i, 0] = _REV[_i] * _MET[_i] * _SCALE
_D64 = _D64.astype(np.float32)
_IDENT = np.eye(128, dtype=np.float32).astype(ml_dtypes.bfloat16)
_SHIFTS = (4 * (np.arange(128) % 8)).astype(np.int32).reshape(128, 1)


class Sem:
    __slots__ = ("h", "n", "name")

    def __init__(self, h, name=""):
        self.h = h
        self.n = 0
        self.name = name


class Phase:

    def __init__(self, name):
        self.name = name
        self.ops = {e: [] for e in _ENGINES}
        self._waits = []

    def add(self, eng, fn, waits=(), incs=(), cincs=(), rb=None):
        waits = tuple(waits)
        incs = tuple(incs)
        cincs = tuple(cincs)
        assert len(incs) <= 1, "one hardware sem update per instruction"
        for s, amt in incs + cincs:
            s.n += amt
        self._waits.extend(waits)
        self.ops[eng].append((waits, fn, incs, cincs, rb))

    def run(self, nc, carriers=None):
        for s, v in self._waits:
            assert v <= s.n, f"{self.name}: wait {v} > final count {s.n} on {s.name}"
        with nc.Block() as blk:
            for eng_name in _ENGINES:
                if not self.ops[eng_name]:
                    continue

                def body(e, _eng=eng_name, _ops=self.ops[eng_name]):
                    for waits, fn, incs, cincs, rb in _ops:
                        for s, v in waits:
                            e.wait_ge(s.h, v)
                        inst = fn(e)
                        if incs:
                            assert inst is not None
                            inst.then_inc(incs[0][0].h, incs[0][1])
                        for s, amt in cincs:
                            if _eng == "vector":
                                e.memset(carriers[0][:, :], 0.0).then_inc(
                                    s.h, amt)
                            elif _eng == "scalar":
                                e.drain().then_inc(s.h, amt)
                            elif _eng == "tensor":
                                raise AssertionError(
                                    "no safe PE carrier; restructure")
                            else:
                                e.nop().then_inc(s.h, amt)

                getattr(blk, eng_name)(body)


def build_program(nphases=3, npair=NPAIR, nlc_cap=None, debug=False, p1_cap=None):
    nc = bass.Bass("TRN2", target_bir_lowering=False)

    x_d = nc.dram_tensor("x", [B, L, DM], F32, kind="ExternalInput").ap()
    xres_d = nc.dram_tensor("xres", [B, L, FS], F32, kind="ExternalInput").ap()
    wq_d = nc.dram_tensor("wq", [FS, DM], F32, kind="ExternalInput").ap()
    wk_d = nc.dram_tensor("wk", [FS, DM], F32, kind="ExternalInput").ap()
    wv_d = nc.dram_tensor("wv", [FS, DM], F32, kind="ExternalInput").ap()
    bq_d = nc.dram_tensor("bq", [FS], F32, kind="ExternalInput").ap()
    bk_d = nc.dram_tensor("bk", [FS], F32, kind="ExternalInput").ap()
    bv_d = nc.dram_tensor("bv", [FS], F32, kind="ExternalInput").ap()
    bwT_d = nc.dram_tensor("bwT", [3, HPC], F32, kind="ExternalInput").ap()
    qw_d = nc.dram_tensor("qw", [DM // 8, FS], I32, kind="ExternalInput").ap()
    scales_d = nc.dram_tensor("scales", [FS], F32, kind="ExternalInput").ap()
    zeros_d = nc.dram_tensor("zeros", [FS], F32, kind="ExternalInput").ap()
    fbias_d = nc.dram_tensor("fbias", [FS], F32, kind="ExternalInput").ap()
    bivkt_d = nc.dram_tensor("bivkt", [3, 64], F32, kind="ExternalInput").ap()
    rev64_d = nc.dram_tensor("rev64", [64, 1], F32, kind="ExternalInput").ap()
    d64_d = nc.dram_tensor("d64", [64, 1], F32, kind="ExternalInput").ap()
    idm_d = nc.dram_tensor("idm", [128, 128], BF16, kind="ExternalInput").ap()
    shifts_d = nc.dram_tensor("shifts", [128, 1], I32, kind="ExternalInput").ap()

    out_d = nc.dram_tensor("out", [B, L, FS], F32, kind="ExternalOutput").ap()
    if debug:
        dbg = {}
        for nm, shp, dt in [
            ("dbg_t64", [64, HPC], F32), ("dbg_bdt", [128, 128], F32),
            ("dbg_wt", [128, 1024], BF16), ("dbg_bprj", [128, 4], F32),
            ("dbg_wf", [128, 1024], BF16), ("dbg_biasb", [128, 8], F32),
            ("dbg_q", [2, 128, L], BF16), ("dbg_k", [2, 128, L], BF16),
            ("dbg_v", [2, 128, L], BF16), ("dbg_g", [256, FS], F32),
            ("dbg_log", [128, L], F32), ("dbg_exp", [128, L], BF16),
            ("dbg_nm", [128, 1], F32), ("dbg_dn", [128, 1], F32),
            ("dbg_sf", [128, HD], F32), ("dbg_vt", [128, 32 * 257], BF16),
            ("dbg_at", [128, 32 * 128], BF16), ("dbg_lnst", [128, 8], F32),
            ("dbg_stats", [128, 48], F32), ("dbg_xtm", [128, DM], BF16),
            ("dbg_xnf", [128, 32 * T1], BF16),
        ]:
            dbg[nm] = nc.dram_tensor(nm, shp, dt, kind="ExternalOutput").ap()

    qS = nc.dram_tensor("qS", [HPC, B, 2, 128, L], BF16, kind="Internal").ap()
    kS = nc.dram_tensor("kS", [HPC, B, 2, 128, L], BF16, kind="Internal").ap()
    vS = nc.dram_tensor("vS", [HPC, B, 2, 128, L], BF16, kind="Internal").ap()
    gS = nc.dram_tensor("gS", [B, L, FS], F32, kind="Internal").ap()
    wfS = nc.dram_tensor("wfS", [128, 32 * FS], BF16, kind="Internal").ap()

    with ExitStack() as top:
        sem = lambda name: Sem(top.enter_context(nc.semaphore(name)), name)
        sbuf = lambda scope, name, shape, dt: scope.enter_context(
            nc.sbuf_tensor(name, shape, dt))
        psum = lambda scope, name, shape: scope.enter_context(
            nc.psum_tensor(name, shape, F32))

        idm = sbuf(top, "idm_sb", [128, 128], BF16)
        carriers = (sbuf(top, "carry_v", [1, 2], F32),
                    sbuf(top, "carry_s", [1, 2], F32),
                    sbuf(top, "carry_rb", [128, 1], F32))

        sA = ExitStack()
        sA.__enter__()
        WT = [sbuf(sA, f"WT{w}", [128, 32 * FS], BF16) for w in range(3)]
        fb_bc = sbuf(sA, "fb_bc", [128, FS], BF16)
        zro = sbuf(sA, "zro", [128, 128], BF16)
        bprj = [sbuf(sA, f"bprj{w}", [128, 4], F32) for w in range(3)]
        epsT = sbuf(sA, "epsT", [128, 1], F32)

        s0 = sem("s0")
        s_c = sem("s_c")
        s_bc = sem("s_bc")
        s_wlp = [sem("s_wl0"), sem("s_wl1")]
        s_wu = sem("s_wu")
        s_wn = sem("s_wn")
        s_tx0 = sem("s_tx0")
        s_txe = sem("s_txe")
        s_qwlp = [sem("s_qwl0"), sem("s_qwl1")]
        s_qwd = sem("s_qwd")
        s_wfs = sem("s_wfs")

        s_xp = [sem("s_x0"), sem("s_x1")]
        s_mv = sem("s_mv")
        s_std = sem("s_std")
        s_ap = sem("s_ap")
        s_lnr = sem("s_lnr")
        s_trd = sem("s_trd")
        s_prj = sem("s_prj")
        s_pje = sem("s_pje")
        s_ffn = sem("s_ffn")
        s_ffe = sem("s_ffe")
        s_ftm = sem("s_ftm")
        s_sil = sem("s_sil")
        s_scq = sem("s_scq")
        s_scg = sem("s_scg")
        s_wf = sem("s_wf")
        s_xnr = sem("s_xnr")
        s_pjd = sem("s_pjd")

        s_pl = sem("s_pl")
        s_gaf = sem("s_gaf")
        s_gae = sem("s_gae")
        s_gbf = sem("s_gbf")
        s_gbe = sem("s_gbe")
        s_max = sem("s_max")
        s_exp = sem("s_exp")
        s_cpd = sem("s_cpd")
        s_dnm = sem("s_dnm")
        s_pmx = sem("s_pmx")
        s_tpf = sem("s_tpf")
        s_tpe = sem("s_tpe")
        s_vtr = sem("s_vtr")
        s_atr = sem("s_atr")
        s_avd = sem("s_avd")
        s_rc = sem("s_rc")
        s_sn = sem("s_sn")
        s_out = sem("s_out")
        s_gx = sem("s_gx")
        s_ost = sem("s_ost")

        with ExitStack() as sP0:
            bw_sb = sbuf(sP0, "bw_sb", [3, HPC], F32)
            bivkt_sb = sbuf(sP0, "bivkt_sb", [3, 64], F32)
            rev64_sb = sbuf(sP0, "rev64_sb", [64, 1], F32)
            d64_sb = sbuf(sP0, "d64_sb", [64, 1], F32)
            shifts_sb = sbuf(sP0, "shifts_sb", [128, 1], I32)
            t64 = sbuf(sP0, "t64", [64, HPC], F32)
            ones_row = sbuf(sP0, "ones_row", [1, 128], F32)
            srow = sbuf(sP0, "srow", [1, FS], F32)
            zrow = sbuf(sP0, "zrow", [1, FS], F32)
            frow = sbuf(sP0, "frow", [1, FS], F32)
            scales_b = sbuf(sP0, "scales_b", [128, FS], F32)
            zeros_b = sbuf(sP0, "zeros_b", [128, FS], F32)
            BDT = [sbuf(sP0, f"BDT{h}", [128, 128], F32) for h in range(HPC)]
            wtmp = [sbuf(sP0, f"wtmp{i}", [128, DM], F32) for i in range(2)]
            wnat = sbuf(sP0, "wnat", [128, DM], BF16)
            qwt = [sbuf(sP0, f"qwt{i}", [128, FS], I32) for i in range(2)]
            qwi = sbuf(sP0, "qwi", [128, FS], I32)
            qff = sbuf(sP0, "qff", [128, FS], F32)
            wfstg = [sbuf(sP0, f"wfstg{i}", [128, FS], BF16) for i in range(2)]
            ps_a = psum(sP0, "ps_a", [128, 512])
            ps_b = psum(sP0, "ps_b", [128, 512])
            ps_t = [psum(sP0, f"ps_t{i}", [128, 512]) for i in range(2)]

            p = Phase("P0a")
            n0 = 0
            loads = [
                (bw_sb[:, :], bwT_d[:, :]),
                (bivkt_sb[:, :], bivkt_d[:, :]),
                (rev64_sb[:, :], rev64_d[:, :]),
                (d64_sb[:, :], d64_d[:, :]),
                (idm[:, :], idm_d[:, :]),
                (shifts_sb[:, :], shifts_d[:, :]),
                (srow[:, :], scales_d.unsqueeze(0)),
                (zrow[:, :], zeros_d.unsqueeze(0)),
                (frow[:, :], fbias_d.unsqueeze(0)),
            ]
            for w, bd in ((0, bq_d), (1, bk_d), (2, bv_d)):
                for fc in range(4):
                    loads.append((bprj[w][:, fc:fc + 1],
                                  bd[fc * 128:(fc + 1) * 128].unsqueeze(1)))
            for dst, src in loads:
                p.add("sync", lambda e, d=dst, s=src: e.dma_start(d, s),
                      incs=[(s0, 16)])
                n0 += 1
            p.add("vector", lambda e: e.memset(ones_row[:, :], 1.0))
            p.add("vector", lambda e: e.memset(zro[:, :], 0.0))
            p.add("vector", lambda e: e.memset(epsT[:, :], EPS))
            p.add("vector", lambda e: e.memset(BDT[0][:, :], 0.0))
            p.add("vector", lambda e: e.memset(BDT[1][:, :], 0.0))
            p.add("sync", lambda e: e.nop(), waits=[(s0, 16 * n0)])
            p.run(nc, carriers)

            p = Phase("P0b")
            p.add("tensor",
                  lambda e: e.matmul(ps_a[0:64, 0:HPC], bivkt_sb[:, :],
                                     bw_sb[:, :], start=True, stop=True),
                  incs=[(s_c, 1)])
            p.add("vector",
                  lambda e: e.tensor_scalar(t64[:, :], ps_a[0:64, 0:HPC],
                                            rev64_sb[:, :], d64_sb[:, :],
                                            ALU.mult, ALU.add),
                  waits=[(s_c, 1)], cincs=[(s_c, 1)])
            for row, dst in ((srow, scales_b), (zrow, zeros_b), (frow, fb_bc)):
                p.add("tensor",
                      lambda e, r=row: e.matmul(ps_b[:, :], ones_row[:, :],
                                                r[:, :], start=True, stop=True),
                      waits=[(s_bc, s_bc.n)], incs=[(s_bc, 1)])
                p.add("vector",
                      lambda e, d=dst: e.tensor_copy(d[:, :], ps_b[:, :]),
                      waits=[(s_bc, s_bc.n)], incs=[(s_bc, 1)])
            p.run(nc, carriers)

            p = Phase("P0c")
            for h in range(HPC):
                for d in range(16):
                    p.add("sync",
                          lambda e, h=h, d=d: e.dma_start(
                              BDT[h][8 * d:8 * d + 8, 8 * d:8 * d + 8],
                              t64[:, h:h + 1]),
                          waits=[(s_c, 2)], incs=[(s0, 16)])
            p.add("sync", lambda e: e.nop(), waits=[(s0, s0.n)])
            if debug:
                p.add("gpsimd", lambda e: e.dma_start(dbg["dbg_t64"], t64[:, :]),
                      waits=[(s_c, 2)], incs=[(s0, 16)])
                p.add("gpsimd", lambda e: e.dma_start(dbg["dbg_bdt"], BDT[0][:, :]),
                      waits=[(s0, s0.n - 16)], incs=[(s0, 16)])
                p.add("gpsimd", lambda e: e.nop(), waits=[(s0, s0.n)])
            p.run(nc, carriers)

            p = Phase("P0d")
            wsrc = (wq_d, wk_d, wv_d)
            fcg = 0

            def tx_group(get_lhs, evac_fn, extra_waits=()):
                bank = s_tx0.n % 2
                for q in range(4):
                    w0 = list(extra_waits) if q == 0 else []
                    if q == 0:
                        w0.append((s_txe, max(0, s_tx0.n - 1)))
                    p.add("tensor",
                          lambda e, bank=bank, q=q, lhs=get_lhs(q): e.matmul(
                              ps_t[bank][:, q * 128:(q + 1) * 128], lhs,
                              idm[:, :], start=True, stop=True),
                          waits=w0, incs=[(s_tx0, 1)] if q == 3 else ())
                p.add("vector", lambda e, bank=bank: evac_fn(e, bank),
                      waits=[(s_tx0, s_tx0.n)], incs=[(s_txe, 1)])

            wn_after = []
            for w in range(3):
                for fc in range(4):
                    par = fcg % 2
                    p.add("sync",
                          lambda e, w=w, fc=fc, par=par: e.dma_start(
                              wtmp[par][:, :],
                              wsrc[w][fc * 128:(fc + 1) * 128, :]),
                          waits=([(s_wn, wn_after[fcg - 2])]
                                 if fcg >= 2 else []),
                          incs=[(s_wlp[par], 16)])
                    if w == 0:
                        h = fc // 2
                        for seg in range(8):
                            pp = ps_a if seg % 2 == 0 else ps_b
                            w0 = [(s_wlp[par], 16 * (fcg // 2 + 1))]
                            if seg == 0:
                                w0.append((s_wn, s_wn.n))
                            else:
                                w0.append((s_wn, max(0, s_wn.n - 1)))
                            p.add("tensor",
                                  lambda e, h=h, par=par, seg=seg, pp=pp:
                                  e.matmul(pp[:, :], BDT[h][:, :],
                                           wtmp[par][:, seg * 512:
                                                     (seg + 1) * 512],
                                           start=True, stop=True),
                                  waits=w0, incs=[(s_c, 1)])
                            wv = [(s_c, s_c.n)]
                            if seg == 0:
                                wv.append((s_tx0, s_tx0.n))
                            p.add("vector",
                                  lambda e, seg=seg, pp=pp: e.tensor_copy(
                                      wnat[:, seg * 512:(seg + 1) * 512],
                                      pp[:, :]),
                                  waits=wv, cincs=[(s_wn, 1)])
                        wn_ready = s_wn.n
                        wn_after.append(wn_ready)
                        p.add("tensor",
                              lambda e, h=h, fc=fc: e.matmul(
                                  ps_b[:, 0:1], BDT[h][:, :],
                                  bprj[0][:, fc:fc + 1], start=True, stop=True),
                              waits=[(s_wn, s_wn.n)], incs=[(s_c, 1)])
                        p.add("vector",
                              lambda e, fc=fc: e.tensor_copy(
                                  bprj[0][:, fc:fc + 1], ps_b[:, 0:1]),
                              waits=[(s_c, s_c.n)], cincs=[(s_wn, 1)])
                    else:
                        p.add("vector",
                              lambda e, par=par: e.tensor_copy(wnat[:, :],
                                                               wtmp[par][:, :]),
                              waits=[(s_wlp[par], 16 * (fcg // 2 + 1)),
                                     (s_tx0, s_tx0.n)],
                              cincs=[(s_wn, 1)])
                        wn_ready = s_wn.n
                        wn_after.append(wn_ready)
                    for g in range(8):
                        def evac(e, bank, w=w, fc=fc, g=g):
                            dst = WT[w][:, :].rearrange(
                                "p (cc f) -> p cc f", f=512)[
                                :, g * 4:(g + 1) * 4, fc * 128:(fc + 1) * 128]
                            src = ps_t[bank][:, :].rearrange(
                                "p (q f) -> p q f", f=128)
                            return e.tensor_copy(dst, src)

                        def get_lhs(q, g=g):
                            cc = g * 4 + q
                            return wnat[:, cc * 128:(cc + 1) * 128]

                        tx_group(get_lhs, evac,
                                 extra_waits=[(s_wn, wn_ready)] if g == 0
                                 else ())
                    fcg += 1
            for cc in range(32):
                par = cc % 2
                for e8 in range(8):
                    p.add("sync",
                          lambda e, cc=cc, e8=e8, par=par: e.dma_start(
                              qwt[par][e8::8, :],
                              qw_d[16 * cc:16 * cc + 16, :]),
                          waits=([(s_qwd, max(0, cc - 1))] if e8 == 0 else []),
                          incs=[(s_qwlp[par], 16)])
                p.add("vector",
                      lambda e, par=par: e.tensor_tensor(
                          qwi[:, :], qwt[par][:, :],
                          shifts_sb[:, :].broadcast_to([128, FS]),
                          ALU.logical_shift_right),
                      waits=[(s_qwlp[par], 16 * 8 * (cc // 2 + 1))])
                p.add("vector",
                      lambda e: e.tensor_single_scalar(qwi[:, :], qwi[:, :],
                                                       15, ALU.bitwise_and))
                p.add("vector", lambda e: e.tensor_copy(qff[:, :], qwi[:, :]))
                p.add("vector",
                      lambda e: e.tensor_mul(qff[:, :], qff[:, :],
                                             scales_b[:, :]))
                p.add("vector",
                      lambda e, par=par: e.tensor_sub(wfstg[par][:, :],
                                                      qff[:, :],
                                                      zeros_b[:, :]),
                      waits=[(s_wfs, 16 * max(0, cc - 1))],
                      cincs=[(s_qwd, 1)])
                p.add("gpsimd",
                      lambda e, cc=cc, par=par: e.dma_start(
                          wfS[:, cc * FS:(cc + 1) * FS], wfstg[par][:, :]),
                      waits=[(s_qwd, s_qwd.n)], incs=[(s_wfs, 16)])
            p.add("gpsimd", lambda e: e.nop(), waits=[(s_wfs, s_wfs.n)])
            p.add("sync", lambda e: e.nop(),
                  waits=[(s_qwlp[0], s_qwlp[0].n), (s_qwlp[1], s_qwlp[1].n),
                         (s_wlp[0], s_wlp[0].n), (s_wlp[1], s_wlp[1].n)])
            if debug:
                p.add("gpsimd", lambda e: e.dma_start(dbg["dbg_wt"],
                      WT[0][:, 0:1024]), incs=[(s0, 16)])
                p.add("gpsimd", lambda e: e.dma_start(dbg["dbg_bprj"],
                      bprj[0][:, :]), incs=[(s0, 16)])
                p.add("gpsimd", lambda e: e.dma_start(dbg["dbg_wf"],
                      wfS[:, 0:1024]), incs=[(s0, 16)])
                p.add("gpsimd", lambda e: e.dma_start(dbg["dbg_biasb"],
                      fb_bc[:, 0:8]), incs=[(s0, 16)])
                p.add("gpsimd", lambda e: e.nop(), waits=[(s0, s0.n)])
            p.run(nc, carriers)

        if nphases < 2:
            sA.__exit__(None, None, None)
            return nc
        with ExitStack() as sP1:
            xbuf = [sbuf(sP1, f"xbuf{i}", [128, DM], F32) for i in range(2)]
            xn_tm = sbuf(sP1, "xn_tm", [128, DM], BF16)
            xn_fm = [sbuf(sP1, f"xn_fm{i}", [128, 32 * T1], BF16)
                     for i in range(2)]
            wfbuf = sbuf(sP1, "wfbuf", [128, 32 * FS], BF16)
            stats = sbuf(sP1, "stats", [128, 48], F32)
            lnst = sbuf(sP1, "lnst", [128, 12], F32)
            lnsta = sbuf(sP1, "lnsta", [128, 2], F32)
            mvb = [lnst[:, 0:2], lnst[:, 2:4]]
            stdb = [lnsta[:, 0:1], lnsta[:, 1:2]]
            rstdb = [lnst[:, 4:5], lnst[:, 5:6]]
            nmrs = [lnst[:, 6:7], lnst[:, 7:8]]
            nm_tmp = lnst[:, 8:9]
            qstg = [sbuf(sP1, f"qstg{i}", [128, T1], BF16) for i in range(2)]
            gstg = [sbuf(sP1, f"gstg{i}", [128, FS], F32) for i in range(2)]
            ps_tx = [psum(sP1, f"ps_tx{i}", [128, 512]) for i in range(2)]
            ps_pj = [psum(sP1, f"ps_pj{i}", [128, 512]) for i in range(4)]
            ps_f = [psum(sP1, f"ps_f{i}", [128, 512]) for i in range(2)]

            SCR = (qS, kS, vS)
            p = Phase("P1")
            txe_base = s_txe.n

            def emit_xload(t):
                b, row0 = t // 32, (t % 32) * 128
                p.add("sync",
                      lambda e, b=b, row0=row0, par=t % 2: e.dma_start(
                          xbuf[par][:, :], x_d[b, row0:row0 + 128, :]),
                      waits=[(s_ap, max(0, t - 1))],
                      incs=[(s_xp[t % 2], 16)])

            def emit_prep(t):
                par = t % 2
                for sseg in range(8):
                    p.add("vector",
                          lambda e, par=par, sseg=sseg: e.bn_stats(
                              stats[:, 6 * sseg:6 * (sseg + 1)],
                              xbuf[par][:, 512 * sseg:512 * (sseg + 1)]),
                          waits=[(s_xp[t % 2], 16 * (t // 2 + 1))]
                          if sseg == 0 else [])
                p.add("vector",
                      lambda e, par=par: e.bn_aggr(mvb[par],
                                                   stats[:, :]),
                      waits=[(s_std, max(0, t - 1))], cincs=[(s_mv, 1)])
                p.add("scalar",
                      lambda e, par=par: e.activation(
                          stdb[par], lnst[:, 2*par+1:2*par+2], AF.Sqrt,
                          bias=epsT[:, :], scale=1.0),
                      waits=[(s_mv, t + 1), (s_ap, max(0, t - 1))],
                      cincs=[(s_std, 1)])
                p.add("vector",
                      lambda e, par=par: e.reciprocal(rstdb[par],
                                                      stdb[par]),
                      waits=[(s_std, t + 1)])

            def emit_prepB(t):
                par = t % 2
                p.add("vector",
                      lambda e, par=par: e.scalar_tensor_tensor(
                          nmrs[par][:, :], lnst[:, 2 * par:2 * par + 1],
                          -1.0, rstdb[par][:, :], ALU.mult, ALU.mult),
                      cincs=[(s_lnr, 1)])
                p.add("scalar",
                      lambda e, par=par: e.activation(
                          xn_tm[:, :], xbuf[par][:, :], AF.Identity,
                          bias=nmrs[par][:, :], scale=rstdb[par][:, :]),
                      waits=[(s_lnr, t + 1),
                             (s_txe, max(0, txe_base + 8 * t))],
                      cincs=[(s_ap, 1)])
                cpar = (t // 2) % 2
                tl = t % 2
                for g in range(8):
                    bank = s_tx0.n % 2
                    for q in range(4):
                        cc = g * 4 + q
                        w0 = []
                        if q == 0:
                            w0 = [(s_txe, max(0, s_tx0.n - 1))]
                            if g == 0:
                                w0.append((s_ap, s_ap.n))
                                if t >= 4:
                                    c_prev = t // 2 - 1
                                    w0.append((s_pje, 12 * c_prev))
                                    w0.append((s_ffe, 2 * c_prev))
                        p.add("tensor",
                              lambda e, bank=bank, q=q, cc=cc: e.matmul(
                                  ps_tx[bank][:, q * 128:(q + 1) * 128],
                                  xn_tm[:, cc * 128:(cc + 1) * 128],
                                  idm[:, :], start=True, stop=True),
                              waits=w0)
                    p.add("tensor",
                          lambda e, bank=bank: e.matmul(
                              ps_tx[bank][:, 3 * 128:4 * 128],
                              idm[:, :], zro[:, :],
                              start=False, stop=True),
                          incs=[(s_tx0, 1)])

                    last_of_chunk = (tl == 1 and g == 7)
                    for q in range(4):
                        cc = g * 4 + q
                        last = (q == 3)

                        def evac(e, cpar=cpar, tl=tl, cc=cc, q=q, bank=bank):
                            dst = xn_fm[cpar][:, cc * T1 + tl * 128:
                                              cc * T1 + (tl + 1) * 128]
                            return e.tensor_copy(
                                dst, ps_tx[bank][:, q * 128:(q + 1) * 128])
                        p.add("vector", evac,
                              waits=[(s_tx0, s_tx0.n)] if q == 0 else [],
                              incs=[(s_txe, 1)] if last else [],
                              cincs=[(s_xnr, 1)]
                              if (last and last_of_chunk) else ())

            def emit_wfload(c):
                for sl in range(4):
                    p.add("sync",
                          lambda e, sl=sl: e.dma_start(
                              wfbuf[:, sl * 8 * FS:(sl + 1) * 8 * FS],
                              wfS[:, sl * 8 * FS:(sl + 1) * 8 * FS]),
                          waits=[(s_ffe, 2 * c)] if sl == 0 else [],
                          incs=[(s_wf, 16)])

            def emit_mm(c):
                b, tc = c // CPB, c % CPB
                t0 = tc * T1
                cpar = c % 2
                xn_ready = txe_base + 16 * (c + 1)
                for w in range(3):
                    for fc in range(4):
                        gidx = c * 12 + w * 4 + fc
                        bank = gidx % 4
                        for cc in range(32):
                            w0 = []
                            if cc == 0:
                                w0 = [(s_xnr, c + 1),
                                      (s_pje, max(0, gidx - 3))]
                            p.add("tensor",
                                  lambda e, w=w, fc=fc, cc=cc, bank=bank,
                                         cpar=cpar: e.matmul(
                                      ps_pj[bank][:, 0:T1],
                                      WT[w][:, cc * 512 + fc * 128:
                                            cc * 512 + (fc + 1) * 128],
                                      xn_fm[cpar][:, cc * T1:(cc + 1) * T1],
                                      start=(cc == 0), stop=(cc == 31)),
                                  waits=w0,
                                  incs=[(s_prj, 1)] if cc == 31 else ())
                        stg = gidx % 2
                        p.add("scalar",
                              lambda e, w=w, fc=fc, bank=bank, stg=stg:
                              e.activation(qstg[stg][:, :],
                                           ps_pj[bank][:, 0:T1], AF.Identity,
                                           bias=bprj[w][:, fc:fc + 1],
                                           scale=1.0),
                              waits=[(s_prj, gidx + 1),
                                     (s_scq, 16 * max(0, gidx - 1))],
                              incs=[(s_pje, 1)], cincs=[(s_pjd, 1)],
                              rb=qstg[stg][:, T1 - 1:T1])
                        p.add("gpsimd",
                              lambda e, w=w, fc=fc, b=b, t0=t0, stg=stg:
                              e.dma_start(
                                  SCR[w][fc // 2, b, fc % 2, :, t0:t0 + T1],
                                  qstg[stg][:, :]),
                              waits=[(s_pjd, gidx + 1)], incs=[(s_scq, 16)])
                for ts in range(2):
                    p.add("tensor",
                          lambda e, ts=ts: e.matmul(
                              ps_f[ts][:, :], idm[:, :], fb_bc[:, :],
                              start=True, stop=False),
                          waits=[(s_xnr, c + 1),
                                 (s_ffe, max(0, 2 * (c - 1) + ts + 1)),
                                 (s_wf, 64 * (c + 1))])
                    for cc in range(32):
                        p.add("tensor",
                              lambda e, cc=cc, ts=ts, cpar=cpar: e.matmul(
                                  ps_f[ts][:, :],
                                  xn_fm[cpar][:, cc * T1 + ts * 128:
                                              cc * T1 + (ts + 1) * 128],
                                  wfbuf[:, cc * FS:(cc + 1) * FS],
                                  start=False, stop=(cc == 31)),
                              incs=[(s_ffn, 1)] if cc == 31 else ())
                for ts in range(2):
                    fseq = 2 * c + ts
                    p.add("scalar",
                          lambda e, ts=ts: e.activation(gstg[ts][:, :],
                                                        ps_f[ts][:, :],
                                                        AF.Silu),
                          waits=[(s_ffn, fseq + 1),
                                 (s_scg, 16 * max(0, fseq - 1))],
                          cincs=[(s_sil, 1), (s_ffe, 1)],
                          rb=gstg[ts][:, FS - 1:FS])
                    p.add("gpsimd",
                          lambda e, ts=ts, b=b, t0=t0: e.dma_start(
                              gS[b, t0 + ts * 128:t0 + (ts + 1) * 128, :],
                              gstg[ts][:, :]),
                          waits=[(s_sil, fseq + 1)], incs=[(s_scg, 16)])

            nt1 = NT1 if p1_cap is None else p1_cap
            for c in range(nt1 + 2):
                if c < nt1:
                    emit_xload(2 * c)
                    emit_xload(2 * c + 1)
                if 1 <= c <= nt1:
                    emit_wfload(c - 1)
                    emit_prep(2 * (c - 1))
                    emit_prep(2 * (c - 1) + 1)
                    emit_prepB(2 * (c - 1))
                    emit_prepB(2 * (c - 1) + 1)
                if c >= 2:
                    emit_mm(c - 2)
            p.add("gpsimd", lambda e: e.nop(),
                  waits=[(s_scq, s_scq.n), (s_scg, s_scg.n)])
            p.add("sync", lambda e: e.nop(),
                  waits=[(s_xp[0], s_xp[0].n), (s_xp[1], s_xp[1].n),
                         (s_wf, s_wf.n)])
            if debug:
                for nm, S in (("dbg_q", qS), ("dbg_k", kS), ("dbg_v", vS)):
                    p.add("gpsimd", lambda e, nm=nm, S=S: e.dma_start(
                        dbg[nm], S[0, 0, :, :, :]), incs=[(s0, 16)])
                p.add("gpsimd", lambda e: e.dma_start(
                    dbg["dbg_g"], gS[0, 0:256, :]), incs=[(s0, 16)])
                p.add("gpsimd", lambda e: e.dma_start(
                    dbg["dbg_lnst"], lnst[:, 0:8]), incs=[(s0, 16)])
                p.add("gpsimd", lambda e: e.dma_start(
                    dbg["dbg_stats"], stats[:, :]), incs=[(s0, 16)])
                p.add("gpsimd", lambda e: e.dma_start(
                    dbg["dbg_xtm"], xn_tm[:, :]), incs=[(s0, 16)])
                p.add("gpsimd", lambda e: e.dma_start(
                    dbg["dbg_xnf"], xn_fm[0][:, :]), incs=[(s0, 16)])
                p.add("gpsimd", lambda e: e.nop(), waits=[(s0, s0.n)])
            p.run(nc, carriers)
        sA.__exit__(None, None, None)

        if nphases < 3:
            return nc
        with ExitStack() as sP2:
            qb = [sbuf(sP2, f"qb{i}", [128, 2 * L], BF16) for i in range(2)]
            kb = [sbuf(sP2, f"kb{i}", [128, 2 * L], BF16) for i in range(2)]
            vb = [sbuf(sP2, f"vb{i}", [128, 2 * L], BF16) for i in range(2)]
            vT = sbuf(sP2, "vT", [128, 32 * 257], BF16)
            logSB = [sbuf(sP2, f"logSB{i}", [128, L], F32) for i in range(2)]
            expSB = [sbuf(sP2, f"expSB{i}", [128, L], BF16) for i in range(2)]
            attnT = [sbuf(sP2, f"attnT{i}", [128, 32 * 128], BF16)
                     for i in range(2)]
            negmx = [sbuf(sP2, f"negmx{i}", [128, 1], F32) for i in range(2)]
            mx4 = sbuf(sP2, "mx4", [128, 4], F32)
            zcol = sbuf(sP2, "zcol", [128, 1], F32)
            denom = [sbuf(sP2, f"denom{i}", [128, 1], F32) for i in range(2)]
            recip = [sbuf(sP2, f"recip{i}", [128, 1], F32) for i in range(2)]
            sfb = [sbuf(sP2, f"sfb{i}", [128, HD], F32) for i in range(2)]
            sgb = [sbuf(sP2, f"sgb{i}", [128, HD], F32) for i in range(2)]
            gt = [sbuf(sP2, f"gt{i}", [128, HD], F32) for i in range(4)]
            xt = [sbuf(sP2, f"xt{i}", [128, HD], F32) for i in range(4)]
            ot = [sbuf(sP2, f"ot{i}", [128, HD], F32) for i in range(4)]
            ps_ga = psum(sP2, "ps_ga", [128, 1024])
            ps_gb = psum(sP2, "ps_gb", [128, 1024])
            ps_tp = [psum(sP2, f"ps_tp{i}", [128, 512]) for i in range(2)]
            ps_av = [psum(sP2, f"ps_av{i}", [128, 512]) for i in range(2)]

            pairs = [(b, h) for b in range(B) for h in range(HPC)][:npair]
            nlc = npair * LCH if nlc_cap is None else nlc_cap
            p = Phase("P2")
            tpe_after = []

            def emit_pair_load(j):
                b, h = pairs[j]
                par = j % 2
                for djc in range(2):
                    for buf, S in ((qb, qS), (kb, kS), (vb, vS)):
                        w0 = []
                        if j >= 2:
                            w0 = [(s_gae, 64 * (j - 1)),
                                  (s_vtr, 17 * (j - 1))]
                        p.add("sync",
                              lambda e, buf=buf, S=S, par=par, h=h, b=b,
                                     djc=djc: e.dma_start(
                                  buf[par][:, djc * L:(djc + 1) * L],
                                  S[h, b, djc, :, :]),
                              waits=w0, incs=[(s_pl, 16)])

            def emit_vtr(j):
                for vt in range(16):
                    bank = s_tpf.n % 2
                    for q in range(4):
                        col = vt * 512 + q * 128
                        mc, djc = col // 256, (col % 256) // 128
                        w0 = []
                        if q == 0:
                            w0 = [(s_tpe, max(0, s_tpf.n - 1)),
                                  (s_pl, 96 * (j + 1))]
                        p.add("tensor",
                              lambda e, bank=bank, q=q, mc=mc, djc=djc,
                                     par=j % 2: e.matmul(
                                  ps_tp[bank][:, q * 128:(q + 1) * 128],
                                  vb[par][:, djc * L + mc * 128:
                                          djc * L + (mc + 1) * 128],
                                  idm[:, :], start=True, stop=True),
                              waits=w0, incs=[(s_tpf, 1)] if q == 3 else ())
                    w1 = [(s_tpf, s_tpf.n)]
                    if vt == 0 and j >= 1:
                        w1.append((s_avd, 32 * j))

                    def vevac(e, vt=vt, bank=bank):
                        dst = vT[:, :].rearrange("p (mc c) -> p mc c", c=257)[
                            :, 2 * vt:2 * vt + 2, 0:256]
                        srcv = ps_tp[bank][:, :].rearrange(
                            "p (t c) -> p t c", c=256)
                        return e.tensor_copy(dst, srcv)
                    p.add("vector", vevac,
                          waits=w1, incs=[(s_tpe, 1)], cincs=[(s_vtr, 1)])
                p.add("vector",
                      lambda e: e.memset(
                          vT[:, :].rearrange("p (mc c) -> p mc c", c=257)[
                              :, :, 256:257], 1.0),
                      cincs=[(s_vtr, 1)])

            def emit_gxload(gi):
                j, li = gi // LCH, gi % LCH
                b, h = pairs[j]
                l0 = li * 128
                p.add("sync",
                      lambda e, b=b, h=h, l0=l0, r=gi % 4: e.dma_start(
                          gt[r][:, :],
                          gS[b, l0:l0 + 128, h * HD:(h + 1) * HD]),
                      waits=[(s_out, max(0, gi - 3))], incs=[(s_gx, 16)])
                p.add("sync",
                      lambda e, b=b, h=h, l0=l0, r=gi % 4: e.dma_start(
                          xt[r][:, :],
                          xres_d[b, l0:l0 + 128, h * HD:(h + 1) * HD]),
                      incs=[(s_gx, 16)])

            def emit_SA(gi):
                j, li = gi // LCH, gi % LCH
                par = j % 2
                gpar = gi % 2
                l0 = li * 128
                for rep in range(2):
                    for half, (ps_g, sf, se) in enumerate(
                            ((ps_ga, s_gaf, s_gae), (ps_gb, s_gbf, s_gbe))):
                        for sub in range(2):
                            mts = rep * 4 + half * 2 + sub
                            m0 = mts * 512
                            for djc in range(2):
                                w0 = []
                                if sub == 0 and djc == 0:
                                    w0 = [(se, sf.n)]
                                    if li == 0:
                                        w0.append((s_pl, 96 * (j + 1)))
                                inc = []
                                if sub == 1 and djc == 1:
                                    inc = [(sf, 1)]
                                p.add("tensor",
                                      lambda e, ps_g=ps_g, sub=sub, djc=djc,
                                             par=par, l0=l0, m0=m0: e.matmul(
                                          ps_g[:, sub * 512:(sub + 1) * 512],
                                          qb[par][:, djc * L + l0:
                                                  djc * L + l0 + 128],
                                          kb[par][:, djc * L + m0:
                                                  djc * L + m0 + 512],
                                          start=(djc == 0), stop=(djc == 1)),
                                      waits=w0, incs=inc)
                        seg = rep * 2 + half
                        w1 = [(sf, sf.n)]
                        if gi >= 2 and seg == 0:
                            w1.append((s_max, gi - 1))
                        p.add("vector",
                              lambda e, ps_g=ps_g, seg=seg: e.reduce_max(
                                  mx4[:, seg:seg + 1], ps_g[:, :], axis=AX.X),
                              waits=w1)
                        p.add("vector",
                              lambda e, ps_g=ps_g, gpar=gpar, seg=seg:
                              e.tensor_copy(logSB[gpar][:, seg * 1024:
                                                        (seg + 1) * 1024],
                                            ps_g[:, :]),
                              incs=[(se, 1)])

                p.add("vector",
                      lambda e, gpar=gpar: e.reduce_max(
                          negmx[gpar][:, :], mx4[:, 0:4], axis=AX.X,
                          negate=True),
                      waits=[(s_exp, max(0, gi - 1))],
                      incs=[], cincs=[(s_max, 1)])

            def emit_SB_tr(gi):
                gpar = gi % 2
                w0 = [(s_max, gi + 1)]
                if gi >= 2:
                    w0.append((s_tpe, tpe_after[gi - 2]))
                p.add("scalar",
                      lambda e, gpar=gpar: e.activation(
                          expSB[gpar][:, :], logSB[gpar][:, :], AF.Exp,
                          bias=negmx[gpar][:, :], scale=1.0),
                      waits=w0, cincs=[(s_exp, 1)])
                for r in range(8):
                    bank = s_tpf.n % 2
                    for q in range(4):
                        mc = r * 4 + q
                        w1 = []
                        if q == 0:
                            w1 = [(s_tpe, max(0, s_tpf.n - 1)),
                                  (s_exp, gi + 1)]
                        inc = [(s_tpf, 1)] if q == 3 else []
                        p.add("tensor",
                              lambda e, bank=bank, q=q, mc=mc, gpar=gpar:
                              e.matmul(
                                  ps_tp[bank][:, q * 128:(q + 1) * 128],
                                  expSB[gpar][:, mc * 128:(mc + 1) * 128],
                                  idm[:, :], start=True, stop=True),
                              waits=w1, incs=inc)
                    w2 = [(s_tpf, s_tpf.n)]
                    if gi >= 2:
                        w2.append((s_avd, gi - 1))
                    p.add("vector",
                          lambda e, r=r, bank=bank, gpar=gpar: e.tensor_copy(
                              attnT[gpar][:, r * 512:(r + 1) * 512],
                              ps_tp[bank][:, :]),
                          waits=w2, incs=[(s_tpe, 1)],
                          cincs=[(s_atr, 1)] if r == 7 else ())
                tpe_after.append(s_tpe.n)

            def emit_SB_av(gi):
                gpar = gi % 2
                j = gi // LCH
                for mc in range(32):
                    w3 = []
                    if mc == 0:
                        w3 = [(s_atr, gi + 1), (s_vtr, 17 * (j + 1)),
                              (s_sn, max(0, gi - 1))]
                    inc = []
                    if mc == 31:
                        inc = [(s_avd, 1)]
                    p.add("tensor",
                          lambda e, mc=mc, gpar=gpar: e.matmul(
                              ps_av[gpar][:, 0:HD + 1],
                              attnT[gpar][:, mc * 128:(mc + 1) * 128],
                              vT[:, mc * 257:(mc + 1) * 257],
                              start=(mc == 0), stop=(mc == 31)),
                          waits=w3, incs=inc)

            def emit_SC(gi):
                j, li = gi // LCH, gi % LCH
                b, h = pairs[j]
                gpar = gi % 2
                r4 = gi % 4
                l0 = li * 128
                if gi + 1 < nlc:
                    ngpar = (gi + 1) % 2
                    p.add("vector",
                          lambda e, ngpar=ngpar: e.reciprocal(
                              recip[ngpar][:, :], ps_av[ngpar][:, HD:HD + 1]),
                          waits=[(s_avd, gi + 2)])
                p.add("vector",
                      lambda e, gpar=gpar: e.tensor_tensor(
                          sfb[gpar][:, :], ps_av[gpar][:, 0:HD],
                          recip[gpar][:, :].broadcast_to([128, HD]),
                          ALU.mult),
                      waits=[(s_avd, gi + 1)],
                      incs=[(s_sn, 1)])
                p.add("vector",
                      lambda e, gpar=gpar, r4=r4: e.tensor_mul(
                          sgb[gpar][:, :], sfb[gpar][:, :], gt[r4][:, :]),
                      waits=[(s_gx, 32 * (gi + 1))])
                p.add("vector",
                      lambda e, gpar=gpar, r4=r4: e.tensor_add(
                          ot[r4][:, :], sgb[gpar][:, :], xt[r4][:, :]),
                      waits=[(s_ost, 16 * max(0, gi - 3))],
                      cincs=[(s_out, 1)])
                p.add("gpsimd",
                      lambda e, b=b, h=h, l0=l0, r4=r4: e.dma_start(
                          out_d[b, l0:l0 + 128, h * HD:(h + 1) * HD],
                          ot[r4][:, :]),
                      waits=[(s_out, gi + 1)], incs=[(s_ost, 16)])

            p.add("vector", lambda e: e.memset(zcol[:, :], 0.0))
            emit_pair_load(0)
            for gi in range(nlc + 2):
                j, li = gi // LCH, gi % LCH
                if 1 <= gi <= nlc:
                    emit_SB_tr(gi - 1)
                if gi < nlc:
                    if li == 16 and j + 1 < npair:
                        emit_pair_load(j + 1)
                    emit_gxload(gi)
                    emit_SA(gi)
                if 1 <= gi <= nlc:
                    emit_SB_av(gi - 1)
                if gi == 1:
                    p.add("vector",
                          lambda e: e.reciprocal(
                              recip[0][:, :], ps_av[0][:, HD:HD + 1]),
                          waits=[(s_avd, 1)])
                if gi < nlc and li == 0:
                    emit_vtr(j)
                if gi >= 2:
                    emit_SC(gi - 2)
            p.add("gpsimd", lambda e: e.nop(), waits=[(s_ost, s_ost.n)])
            p.add("sync", lambda e: e.nop(),
                  waits=[(s_pl, s_pl.n), (s_gx, s_gx.n)])
            if debug:
                for nm, src_ap in (("dbg_log", logSB[0][:, :]),
                                   ("dbg_exp", expSB[0][:, :]),
                                   ("dbg_nm", negmx[0][:, :]),
                                   ("dbg_dn", recip[0][:, :]),
                                   ("dbg_sf", sfb[0][:, :]),
                                   ("dbg_vt", vT[:, :]),
                                   ("dbg_at", attnT[0][:, :])):
                    p.add("gpsimd", lambda e, nm=nm, s_=src_ap: e.dma_start(
                        dbg[nm], s_), incs=[(s0, 16)])
                p.add("gpsimd", lambda e: e.nop(), waits=[(s0, s0.n)])
            p.run(nc, carriers)

    return nc


def kernel(L0_attn_x, L0_attn_Wq, L0_attn_bq, L0_attn_Wk, L0_attn_bk,
           L0_attn_Wv, L0_attn_bv, L0_attn_bias_weight, L0_ffn_qweight,
           L0_ffn_scales, L0_ffn_zeros, L0_ffn_bias):
    x = np.ascontiguousarray(np.asarray(L0_attn_x, dtype=np.float32))
    Wq = np.asarray(L0_attn_Wq, dtype=np.float32)
    Wk = np.asarray(L0_attn_Wk, dtype=np.float32)
    Wv = np.asarray(L0_attn_Wv, dtype=np.float32)
    bq = np.asarray(L0_attn_bq, dtype=np.float32)
    bk = np.asarray(L0_attn_bk, dtype=np.float32)
    bv = np.asarray(L0_attn_bv, dtype=np.float32)
    bw = np.asarray(L0_attn_bias_weight, dtype=np.float32)
    qw = np.asarray(L0_ffn_qweight, dtype=np.int32)
    sc = np.asarray(L0_ffn_scales, dtype=np.float32)
    zr = np.asarray(L0_ffn_zeros, dtype=np.float32)
    fb = np.asarray(L0_ffn_bias, dtype=np.float32)

    nc = build_program()
    in_maps = []
    for c in range(NCORES):
        f0, f1 = FS * c, FS * (c + 1)
        in_maps.append({
            "x": x,
            "xres": np.ascontiguousarray(x[:, :, f0:f1]),
            "wq": np.ascontiguousarray(Wq[f0:f1, :]),
            "wk": np.ascontiguousarray(Wk[f0:f1, :]),
            "wv": np.ascontiguousarray(Wv[f0:f1, :]),
            "bq": np.ascontiguousarray(bq[f0:f1]),
            "bk": np.ascontiguousarray(bk[f0:f1]),
            "bv": np.ascontiguousarray(bv[f0:f1]),
            "bwT": np.ascontiguousarray(bw[HPC * c:HPC * (c + 1), :].T),
            "qw": np.ascontiguousarray(qw[:, f0:f1]),
            "scales": np.ascontiguousarray(sc[f0:f1, 0]),
            "zeros": np.ascontiguousarray(zr[f0:f1, 0]),
            "fbias": np.ascontiguousarray(fb[f0:f1]),
            "bivkt": _BIVKT,
            "rev64": _REV64,
            "d64": _D64,
            "idm": _IDENT,
            "shifts": _SHIFTS,
        })
    res = run_bass_kernel_spmd(nc, in_maps, core_ids=list(range(NCORES)))
    return np.concatenate([np.asarray(r["out"]) for r in res.results], axis=2)


# version 14
# speedup vs baseline: 1.0652x; 1.0053x over previous
import numpy as np
import ml_dtypes
import concourse.bass as bass
import concourse.mybir as mybir
from concourse.bass_utils import run_bass_kernel_spmd
from contextlib import ExitStack

F32 = mybir.dt.float32
BF16 = mybir.dt.bfloat16
I32 = mybir.dt.int32
AF = mybir.ActivationFunctionType
ALU = mybir.AluOpType
AX = mybir.AxisListType

B, L, DM = 4, 4096, 4096
NH, CD, NB = 16, 32, 8
HD = CD * NB
NCORES = 8
HPC = NH // NCORES
FS = HPC * HD
T1 = 256
NT1 = (B * L) // T1
CPB = L // T1
NPAIR = B * HPC
LCH = L // 128
NLC = NPAIR * LCH
EPS = 1e-5

_ENGINES = ("sync", "gpsimd", "vector", "scalar", "tensor")


def _tables():
    g = [1.0, 1.0, 1.0]
    dim, n = 3, 8
    masks = sorted(range(n), key=lambda m: (bin(m).count("1"), m))
    inv = {m: i for i, m in enumerate(masks)}
    C = np.zeros((n, n, n), dtype=np.float64)
    for i, a in enumerate(masks):
        for j, b_ in enumerate(masks):
            sw, at = 0, a >> 1
            while at:
                sw += bin(at & b_).count("1")
                at >>= 1
            s = -1.0 if sw & 1 else 1.0
            for t in range(dim):
                if a >> t & 1 and b_ >> t & 1:
                    s *= g[t]
            C[i, j, inv[a ^ b_]] = s
    grades = np.array([bin(m).count("1") for m in masks])
    rev = np.where(grades * (grades - 1) // 2 % 2 == 1, -1.0, 1.0)
    met = np.array([np.prod([g[t] for t in range(dim) if m >> t & 1] + [1.0])
                    for m in masks])
    return C, rev, met


_C_TAB, _REV, _MET = _tables()
_BIV_IDX = [4, 5, 6]
_SCALE = 1.0 / np.sqrt(CD * NB)

_BIVKT = np.ascontiguousarray(
    _C_TAB[:, :, _BIV_IDX].transpose(2, 0, 1).reshape(3, 64)).astype(np.float32)
_REV64 = np.repeat(_REV, 8).reshape(64, 1).astype(np.float32)
_D64 = np.zeros((64, 1), dtype=np.float64)
for _i in range(8):
    _D64[_i * 8 + _i, 0] = _REV[_i] * _MET[_i] * _SCALE
_D64 = _D64.astype(np.float32)
_IDENT = np.eye(128, dtype=np.float32).astype(ml_dtypes.bfloat16)
_SHIFTS = (4 * (np.arange(128) % 8)).astype(np.int32).reshape(128, 1)


class Sem:
    __slots__ = ("h", "n", "name")

    def __init__(self, h, name=""):
        self.h = h
        self.n = 0
        self.name = name


class Phase:

    def __init__(self, name):
        self.name = name
        self.ops = {e: [] for e in _ENGINES}
        self._waits = []

    def add(self, eng, fn, waits=(), incs=(), cincs=(), rb=None):
        waits = tuple(waits)
        incs = tuple(incs)
        cincs = tuple(cincs)
        assert len(incs) <= 1, "one hardware sem update per instruction"
        for s, amt in incs + cincs:
            s.n += amt
        self._waits.extend(waits)
        self.ops[eng].append((waits, fn, incs, cincs, rb))

    def run(self, nc, carriers=None):
        for s, v in self._waits:
            assert v <= s.n, f"{self.name}: wait {v} > final count {s.n} on {s.name}"
        with nc.Block() as blk:
            for eng_name in _ENGINES:
                if not self.ops[eng_name]:
                    continue

                def body(e, _eng=eng_name, _ops=self.ops[eng_name]):
                    for waits, fn, incs, cincs, rb in _ops:
                        for s, v in waits:
                            e.wait_ge(s.h, v)
                        inst = fn(e)
                        if incs:
                            assert inst is not None
                            inst.then_inc(incs[0][0].h, incs[0][1])
                        for s, amt in cincs:
                            if _eng == "vector":
                                e.memset(carriers[0][:, :], 0.0).then_inc(
                                    s.h, amt)
                            elif _eng == "scalar":
                                e.drain().then_inc(s.h, amt)
                            elif _eng == "tensor":
                                raise AssertionError(
                                    "no safe PE carrier; restructure")
                            else:
                                e.nop().then_inc(s.h, amt)

                getattr(blk, eng_name)(body)


def build_program(nphases=3, npair=NPAIR, nlc_cap=None, debug=False, p1_cap=None):
    nc = bass.Bass("TRN2", target_bir_lowering=False)

    x_d = nc.dram_tensor("x", [B, L, DM], F32, kind="ExternalInput").ap()
    xres_d = nc.dram_tensor("xres", [B, L, FS], F32, kind="ExternalInput").ap()
    wq_d = nc.dram_tensor("wq", [FS, DM], F32, kind="ExternalInput").ap()
    wk_d = nc.dram_tensor("wk", [FS, DM], F32, kind="ExternalInput").ap()
    wv_d = nc.dram_tensor("wv", [FS, DM], F32, kind="ExternalInput").ap()
    bq_d = nc.dram_tensor("bq", [FS], F32, kind="ExternalInput").ap()
    bk_d = nc.dram_tensor("bk", [FS], F32, kind="ExternalInput").ap()
    bv_d = nc.dram_tensor("bv", [FS], F32, kind="ExternalInput").ap()
    bwT_d = nc.dram_tensor("bwT", [3, HPC], F32, kind="ExternalInput").ap()
    qw_d = nc.dram_tensor("qw", [DM // 8, FS], I32, kind="ExternalInput").ap()
    scales_d = nc.dram_tensor("scales", [FS], F32, kind="ExternalInput").ap()
    zeros_d = nc.dram_tensor("zeros", [FS], F32, kind="ExternalInput").ap()
    fbias_d = nc.dram_tensor("fbias", [FS], F32, kind="ExternalInput").ap()
    bivkt_d = nc.dram_tensor("bivkt", [3, 64], F32, kind="ExternalInput").ap()
    rev64_d = nc.dram_tensor("rev64", [64, 1], F32, kind="ExternalInput").ap()
    d64_d = nc.dram_tensor("d64", [64, 1], F32, kind="ExternalInput").ap()
    idm_d = nc.dram_tensor("idm", [128, 128], BF16, kind="ExternalInput").ap()
    shifts_d = nc.dram_tensor("shifts", [128, 1], I32, kind="ExternalInput").ap()

    out_d = nc.dram_tensor("out", [B, L, FS], F32, kind="ExternalOutput").ap()
    if debug:
        dbg = {}
        for nm, shp, dt in [
            ("dbg_t64", [64, HPC], F32), ("dbg_bdt", [128, 128], F32),
            ("dbg_wt", [128, 1024], BF16), ("dbg_bprj", [128, 4], F32),
            ("dbg_wf", [128, 1024], BF16), ("dbg_biasb", [128, 8], F32),
            ("dbg_q", [2, 128, L], BF16), ("dbg_k", [2, 128, L], BF16),
            ("dbg_v", [2, 128, L], BF16), ("dbg_g", [256, FS], F32),
            ("dbg_log", [128, L], F32), ("dbg_exp", [128, L], BF16),
            ("dbg_nm", [128, 1], F32), ("dbg_dn", [128, 1], F32),
            ("dbg_sf", [128, HD], F32), ("dbg_vt", [128, 32 * 257], BF16),
            ("dbg_at", [128, 32 * 128], BF16), ("dbg_lnst", [128, 8], F32),
            ("dbg_stats", [128, 48], F32), ("dbg_xtm", [128, DM], BF16),
            ("dbg_xnf", [128, 32 * T1], BF16),
        ]:
            dbg[nm] = nc.dram_tensor(nm, shp, dt, kind="ExternalOutput").ap()

    qS = nc.dram_tensor("qS", [HPC, B, 2, 128, L], BF16, kind="Internal").ap()
    kS = nc.dram_tensor("kS", [HPC, B, 2, 128, L], BF16, kind="Internal").ap()
    vS = nc.dram_tensor("vS", [HPC, B, 2, 128, L], BF16, kind="Internal").ap()
    gS = nc.dram_tensor("gS", [B, L, FS], F32, kind="Internal").ap()
    wfS = nc.dram_tensor("wfS", [128, 32 * FS], BF16, kind="Internal").ap()

    with ExitStack() as top:
        sem = lambda name: Sem(top.enter_context(nc.semaphore(name)), name)
        sbuf = lambda scope, name, shape, dt: scope.enter_context(
            nc.sbuf_tensor(name, shape, dt))
        psum = lambda scope, name, shape: scope.enter_context(
            nc.psum_tensor(name, shape, F32))

        idm = sbuf(top, "idm_sb", [128, 128], BF16)
        carriers = (sbuf(top, "carry_v", [1, 2], F32),
                    sbuf(top, "carry_s", [1, 2], F32),
                    sbuf(top, "carry_rb", [128, 1], F32))

        sA = ExitStack()
        sA.__enter__()
        WT = [sbuf(sA, f"WT{w}", [128, 32 * FS], BF16) for w in range(3)]
        fb_bc = sbuf(sA, "fb_bc", [128, FS], BF16)
        zro = sbuf(sA, "zro", [128, 128], BF16)
        bprj = [sbuf(sA, f"bprj{w}", [128, 4], F32) for w in range(3)]
        epsT = sbuf(sA, "epsT", [128, 1], F32)

        s0 = sem("s0")
        s_c = sem("s_c")
        s_bc = sem("s_bc")
        s_wl = sem("s_wl")
        s_wu = sem("s_wu")
        s_wn = sem("s_wn")
        s_tx0 = sem("s_tx0")
        s_txe = sem("s_txe")
        s_qwl = sem("s_qwl")
        s_qwd = sem("s_qwd")
        s_wfs = sem("s_wfs")

        s_xp = [sem("s_x0"), sem("s_x1")]
        s_mv = sem("s_mv")
        s_std = sem("s_std")
        s_ap = sem("s_ap")
        s_lnr = sem("s_lnr")
        s_trd = sem("s_trd")
        s_prj = sem("s_prj")
        s_pje = sem("s_pje")
        s_ffn = sem("s_ffn")
        s_ffe = sem("s_ffe")
        s_ftm = sem("s_ftm")
        s_sil = sem("s_sil")
        s_scq = sem("s_scq")
        s_scg = sem("s_scg")
        s_wf = sem("s_wf")
        s_xnr = sem("s_xnr")
        s_pjd = sem("s_pjd")

        s_pl = sem("s_pl")
        s_gaf = sem("s_gaf")
        s_gae = sem("s_gae")
        s_gbf = sem("s_gbf")
        s_gbe = sem("s_gbe")
        s_max = sem("s_max")
        s_exp = sem("s_exp")
        s_cpd = sem("s_cpd")
        s_dnm = sem("s_dnm")
        s_pmx = sem("s_pmx")
        s_tpf = sem("s_tpf")
        s_tpe = sem("s_tpe")
        s_vtr = sem("s_vtr")
        s_atr = sem("s_atr")
        s_avd = sem("s_avd")
        s_rc = sem("s_rc")
        s_sn = sem("s_sn")
        s_out = sem("s_out")
        s_gx = sem("s_gx")
        s_ost = sem("s_ost")

        with ExitStack() as sP0:
            bw_sb = sbuf(sP0, "bw_sb", [3, HPC], F32)
            bivkt_sb = sbuf(sP0, "bivkt_sb", [3, 64], F32)
            rev64_sb = sbuf(sP0, "rev64_sb", [64, 1], F32)
            d64_sb = sbuf(sP0, "d64_sb", [64, 1], F32)
            shifts_sb = sbuf(sP0, "shifts_sb", [128, 1], I32)
            t64 = sbuf(sP0, "t64", [64, HPC], F32)
            ones_row = sbuf(sP0, "ones_row", [1, 128], F32)
            srow = sbuf(sP0, "srow", [1, FS], F32)
            zrow = sbuf(sP0, "zrow", [1, FS], F32)
            frow = sbuf(sP0, "frow", [1, FS], F32)
            scales_b = sbuf(sP0, "scales_b", [128, FS], F32)
            zeros_b = sbuf(sP0, "zeros_b", [128, FS], F32)
            BDT = [sbuf(sP0, f"BDT{h}", [128, 128], F32) for h in range(HPC)]
            wtmp = [sbuf(sP0, f"wtmp{i}", [128, DM], F32) for i in range(2)]
            wnat = sbuf(sP0, "wnat", [128, DM], BF16)
            qwt = [sbuf(sP0, f"qwt{i}", [128, FS], I32) for i in range(2)]
            qwi = sbuf(sP0, "qwi", [128, FS], I32)
            qff = sbuf(sP0, "qff", [128, FS], F32)
            wfstg = [sbuf(sP0, f"wfstg{i}", [128, FS], BF16) for i in range(2)]
            ps_a = psum(sP0, "ps_a", [128, 512])
            ps_b = psum(sP0, "ps_b", [128, 512])
            ps_t = [psum(sP0, f"ps_t{i}", [128, 512]) for i in range(2)]

            p = Phase("P0a")
            n0 = 0
            loads = [
                (bw_sb[:, :], bwT_d[:, :]),
                (bivkt_sb[:, :], bivkt_d[:, :]),
                (rev64_sb[:, :], rev64_d[:, :]),
                (d64_sb[:, :], d64_d[:, :]),
                (idm[:, :], idm_d[:, :]),
                (shifts_sb[:, :], shifts_d[:, :]),
                (srow[:, :], scales_d.unsqueeze(0)),
                (zrow[:, :], zeros_d.unsqueeze(0)),
                (frow[:, :], fbias_d.unsqueeze(0)),
            ]
            for w, bd in ((0, bq_d), (1, bk_d), (2, bv_d)):
                for fc in range(4):
                    loads.append((bprj[w][:, fc:fc + 1],
                                  bd[fc * 128:(fc + 1) * 128].unsqueeze(1)))
            for dst, src in loads:
                p.add("sync", lambda e, d=dst, s=src: e.dma_start(d, s),
                      incs=[(s0, 16)])
                n0 += 1
            p.add("vector", lambda e: e.memset(ones_row[:, :], 1.0))
            p.add("vector", lambda e: e.memset(zro[:, :], 0.0))
            p.add("vector", lambda e: e.memset(epsT[:, :], EPS))
            p.add("vector", lambda e: e.memset(BDT[0][:, :], 0.0))
            p.add("vector", lambda e: e.memset(BDT[1][:, :], 0.0))
            p.add("sync", lambda e: e.nop(), waits=[(s0, 16 * n0)])
            p.run(nc, carriers)

            p = Phase("P0b")
            p.add("tensor",
                  lambda e: e.matmul(ps_a[0:64, 0:HPC], bivkt_sb[:, :],
                                     bw_sb[:, :], start=True, stop=True),
                  incs=[(s_c, 1)])
            p.add("vector",
                  lambda e: e.tensor_scalar(t64[:, :], ps_a[0:64, 0:HPC],
                                            rev64_sb[:, :], d64_sb[:, :],
                                            ALU.mult, ALU.add),
                  waits=[(s_c, 1)], cincs=[(s_c, 1)])
            for row, dst in ((srow, scales_b), (zrow, zeros_b), (frow, fb_bc)):
                p.add("tensor",
                      lambda e, r=row: e.matmul(ps_b[:, :], ones_row[:, :],
                                                r[:, :], start=True, stop=True),
                      waits=[(s_bc, s_bc.n)], incs=[(s_bc, 1)])
                p.add("vector",
                      lambda e, d=dst: e.tensor_copy(d[:, :], ps_b[:, :]),
                      waits=[(s_bc, s_bc.n)], incs=[(s_bc, 1)])
            p.run(nc, carriers)

            p = Phase("P0c")
            for h in range(HPC):
                for d in range(16):
                    p.add("sync",
                          lambda e, h=h, d=d: e.dma_start(
                              BDT[h][8 * d:8 * d + 8, 8 * d:8 * d + 8],
                              t64[:, h:h + 1]),
                          waits=[(s_c, 2)], incs=[(s0, 16)])
            p.add("sync", lambda e: e.nop(), waits=[(s0, s0.n)])
            if debug:
                p.add("gpsimd", lambda e: e.dma_start(dbg["dbg_t64"], t64[:, :]),
                      waits=[(s_c, 2)], incs=[(s0, 16)])
                p.add("gpsimd", lambda e: e.dma_start(dbg["dbg_bdt"], BDT[0][:, :]),
                      waits=[(s0, s0.n - 16)], incs=[(s0, 16)])
                p.add("gpsimd", lambda e: e.nop(), waits=[(s0, s0.n)])
            p.run(nc, carriers)

            p = Phase("P0d")
            wsrc = (wq_d, wk_d, wv_d)
            fcg = 0

            def tx_group(get_lhs, evac_fn, extra_waits=()):
                bank = s_tx0.n % 2
                for q in range(4):
                    w0 = list(extra_waits) if q == 0 else []
                    if q == 0:
                        w0.append((s_txe, max(0, s_tx0.n - 1)))
                    p.add("tensor",
                          lambda e, bank=bank, q=q, lhs=get_lhs(q): e.matmul(
                              ps_t[bank][:, q * 128:(q + 1) * 128], lhs,
                              idm[:, :], start=True, stop=True),
                          waits=w0, incs=[(s_tx0, 1)] if q == 3 else ())
                p.add("vector", lambda e, bank=bank: evac_fn(e, bank),
                      waits=[(s_tx0, s_tx0.n)], incs=[(s_txe, 1)])

            wn_after = []
            for w in range(3):
                for fc in range(4):
                    par = fcg % 2
                    p.add("sync",
                          lambda e, w=w, fc=fc, par=par: e.dma_start(
                              wtmp[par][:, :],
                              wsrc[w][fc * 128:(fc + 1) * 128, :]),
                          waits=([(s_wn, wn_after[fcg - 2])]
                                 if fcg >= 2 else []),
                          incs=[(s_wl, 16)])
                    if w == 0:
                        h = fc // 2
                        for seg in range(8):
                            pp = ps_a if seg % 2 == 0 else ps_b
                            w0 = [(s_wl, 16 * (fcg + 1))]
                            if seg == 0:
                                w0.append((s_wn, s_wn.n))
                            else:
                                w0.append((s_wn, max(0, s_wn.n - 1)))
                            p.add("tensor",
                                  lambda e, h=h, par=par, seg=seg, pp=pp:
                                  e.matmul(pp[:, :], BDT[h][:, :],
                                           wtmp[par][:, seg * 512:
                                                     (seg + 1) * 512],
                                           start=True, stop=True),
                                  waits=w0, incs=[(s_c, 1)])
                            wv = [(s_c, s_c.n)]
                            if seg == 0:
                                wv.append((s_tx0, s_tx0.n))
                            p.add("vector",
                                  lambda e, seg=seg, pp=pp: e.tensor_copy(
                                      wnat[:, seg * 512:(seg + 1) * 512],
                                      pp[:, :]),
                                  waits=wv, cincs=[(s_wn, 1)])
                        wn_ready = s_wn.n
                        wn_after.append(wn_ready)
                        p.add("tensor",
                              lambda e, h=h, fc=fc: e.matmul(
                                  ps_b[:, 0:1], BDT[h][:, :],
                                  bprj[0][:, fc:fc + 1], start=True, stop=True),
                              waits=[(s_wn, s_wn.n)], incs=[(s_c, 1)])
                        p.add("vector",
                              lambda e, fc=fc: e.tensor_copy(
                                  bprj[0][:, fc:fc + 1], ps_b[:, 0:1]),
                              waits=[(s_c, s_c.n)], cincs=[(s_wn, 1)])
                    else:
                        p.add("vector",
                              lambda e, par=par: e.tensor_copy(wnat[:, :],
                                                               wtmp[par][:, :]),
                              waits=[(s_wl, 16 * (fcg + 1)),
                                     (s_tx0, s_tx0.n)],
                              cincs=[(s_wn, 1)])
                        wn_ready = s_wn.n
                        wn_after.append(wn_ready)
                    for g in range(8):
                        def evac(e, bank, w=w, fc=fc, g=g):
                            dst = WT[w][:, :].rearrange(
                                "p (cc f) -> p cc f", f=512)[
                                :, g * 4:(g + 1) * 4, fc * 128:(fc + 1) * 128]
                            src = ps_t[bank][:, :].rearrange(
                                "p (q f) -> p q f", f=128)
                            return e.tensor_copy(dst, src)

                        def get_lhs(q, g=g):
                            cc = g * 4 + q
                            return wnat[:, cc * 128:(cc + 1) * 128]

                        tx_group(get_lhs, evac,
                                 extra_waits=[(s_wn, wn_ready)] if g == 0
                                 else ())
                    fcg += 1
            for cc in range(32):
                par = cc % 2
                for e8 in range(8):
                    p.add("sync",
                          lambda e, cc=cc, e8=e8, par=par: e.dma_start(
                              qwt[par][e8::8, :],
                              qw_d[16 * cc:16 * cc + 16, :]),
                          waits=([(s_qwd, max(0, cc - 1))] if e8 == 0 else []),
                          incs=[(s_qwl, 16)])
                p.add("vector",
                      lambda e, par=par: e.tensor_tensor(
                          qwi[:, :], qwt[par][:, :],
                          shifts_sb[:, :].broadcast_to([128, FS]),
                          ALU.logical_shift_right),
                      waits=[(s_qwl, 16 * 8 * (cc + 1))])
                p.add("vector",
                      lambda e: e.tensor_single_scalar(qwi[:, :], qwi[:, :],
                                                       15, ALU.bitwise_and))
                p.add("vector", lambda e: e.tensor_copy(qff[:, :], qwi[:, :]))
                p.add("vector",
                      lambda e: e.tensor_mul(qff[:, :], qff[:, :],
                                             scales_b[:, :]))
                p.add("vector",
                      lambda e, par=par: e.tensor_sub(wfstg[par][:, :],
                                                      qff[:, :],
                                                      zeros_b[:, :]),
                      waits=[(s_wfs, 16 * max(0, cc - 1))],
                      cincs=[(s_qwd, 1)])
                p.add("gpsimd",
                      lambda e, cc=cc, par=par: e.dma_start(
                          wfS[:, cc * FS:(cc + 1) * FS], wfstg[par][:, :]),
                      waits=[(s_qwd, s_qwd.n)], incs=[(s_wfs, 16)])
            p.add("gpsimd", lambda e: e.nop(), waits=[(s_wfs, s_wfs.n)])
            p.add("sync", lambda e: e.nop(),
                  waits=[(s_qwl, s_qwl.n), (s_wl, s_wl.n)])
            if debug:
                p.add("gpsimd", lambda e: e.dma_start(dbg["dbg_wt"],
                      WT[0][:, 0:1024]), incs=[(s0, 16)])
                p.add("gpsimd", lambda e: e.dma_start(dbg["dbg_bprj"],
                      bprj[0][:, :]), incs=[(s0, 16)])
                p.add("gpsimd", lambda e: e.dma_start(dbg["dbg_wf"],
                      wfS[:, 0:1024]), incs=[(s0, 16)])
                p.add("gpsimd", lambda e: e.dma_start(dbg["dbg_biasb"],
                      fb_bc[:, 0:8]), incs=[(s0, 16)])
                p.add("gpsimd", lambda e: e.nop(), waits=[(s0, s0.n)])
            p.run(nc, carriers)

        if nphases < 2:
            sA.__exit__(None, None, None)
            return nc
        with ExitStack() as sP1:
            xbuf = [sbuf(sP1, f"xbuf{i}", [128, DM], F32) for i in range(2)]
            xn_tm = sbuf(sP1, "xn_tm", [128, DM], BF16)
            xn_fm = [sbuf(sP1, f"xn_fm{i}", [128, 32 * T1], BF16)
                     for i in range(2)]
            wfbuf = sbuf(sP1, "wfbuf", [128, 32 * FS], BF16)
            stats = sbuf(sP1, "stats", [128, 48], F32)
            lnst = sbuf(sP1, "lnst", [128, 12], F32)
            lnsta = sbuf(sP1, "lnsta", [128, 2], F32)
            mvb = [lnst[:, 0:2], lnst[:, 2:4]]
            stdb = [lnsta[:, 0:1], lnsta[:, 1:2]]
            rstdb = [lnst[:, 4:5], lnst[:, 5:6]]
            nmrs = [lnst[:, 6:7], lnst[:, 7:8]]
            nm_tmp = lnst[:, 8:9]
            qstg = [sbuf(sP1, f"qstg{i}", [128, T1], BF16) for i in range(2)]
            gstg = [sbuf(sP1, f"gstg{i}", [128, FS], F32) for i in range(2)]
            ps_tx = [psum(sP1, f"ps_tx{i}", [128, 512]) for i in range(2)]
            ps_pj = [psum(sP1, f"ps_pj{i}", [128, 512]) for i in range(4)]
            ps_f = [psum(sP1, f"ps_f{i}", [128, 512]) for i in range(2)]

            SCR = (qS, kS, vS)
            p = Phase("P1")
            txe_base = s_txe.n

            def emit_xload(t):
                b, row0 = t // 32, (t % 32) * 128
                p.add("sync",
                      lambda e, b=b, row0=row0, par=t % 2: e.dma_start(
                          xbuf[par][:, :], x_d[b, row0:row0 + 128, :]),
                      waits=[(s_ap, max(0, t - 1))],
                      incs=[(s_xp[t % 2], 16)])

            def emit_prep(t):
                par = t % 2
                for sseg in range(8):
                    p.add("vector",
                          lambda e, par=par, sseg=sseg: e.bn_stats(
                              stats[:, 6 * sseg:6 * (sseg + 1)],
                              xbuf[par][:, 512 * sseg:512 * (sseg + 1)]),
                          waits=[(s_xp[t % 2], 16 * (t // 2 + 1))]
                          if sseg == 0 else [])
                p.add("vector",
                      lambda e, par=par: e.bn_aggr(mvb[par],
                                                   stats[:, :]),
                      waits=[(s_std, max(0, t - 1))], cincs=[(s_mv, 1)])
                p.add("scalar",
                      lambda e, par=par: e.activation(
                          stdb[par], lnst[:, 2*par+1:2*par+2], AF.Sqrt,
                          bias=epsT[:, :], scale=1.0),
                      waits=[(s_mv, t + 1), (s_ap, max(0, t - 1))],
                      cincs=[(s_std, 1)])
                p.add("vector",
                      lambda e, par=par: e.reciprocal(rstdb[par],
                                                      stdb[par]),
                      waits=[(s_std, t + 1)])

            def emit_prepB(t):
                par = t % 2
                p.add("vector",
                      lambda e, par=par: e.scalar_tensor_tensor(
                          nmrs[par][:, :], lnst[:, 2 * par:2 * par + 1],
                          -1.0, rstdb[par][:, :], ALU.mult, ALU.mult),
                      cincs=[(s_lnr, 1)])
                p.add("scalar",
                      lambda e, par=par: e.activation(
                          xn_tm[:, :], xbuf[par][:, :], AF.Identity,
                          bias=nmrs[par][:, :], scale=rstdb[par][:, :]),
                      waits=[(s_lnr, t + 1),
                             (s_txe, max(0, txe_base + 8 * t))],
                      cincs=[(s_ap, 1)])
                cpar = (t // 2) % 2
                tl = t % 2
                for g in range(8):
                    bank = s_tx0.n % 2
                    for q in range(4):
                        cc = g * 4 + q
                        w0 = []
                        if q == 0:
                            w0 = [(s_txe, max(0, s_tx0.n - 1))]
                            if g == 0:
                                w0.append((s_ap, s_ap.n))
                                if t >= 4:
                                    c_prev = t // 2 - 1
                                    w0.append((s_pje, 12 * c_prev))
                                    w0.append((s_ffe, 2 * c_prev))
                        p.add("tensor",
                              lambda e, bank=bank, q=q, cc=cc: e.matmul(
                                  ps_tx[bank][:, q * 128:(q + 1) * 128],
                                  xn_tm[:, cc * 128:(cc + 1) * 128],
                                  idm[:, :], start=True, stop=True),
                              waits=w0)
                    p.add("tensor",
                          lambda e, bank=bank: e.matmul(
                              ps_tx[bank][:, 3 * 128:4 * 128],
                              idm[:, :], zro[:, :],
                              start=False, stop=True),
                          incs=[(s_tx0, 1)])

                    last_of_chunk = (tl == 1 and g == 7)
                    for q in range(4):
                        cc = g * 4 + q
                        last = (q == 3)

                        def evac(e, cpar=cpar, tl=tl, cc=cc, q=q, bank=bank):
                            dst = xn_fm[cpar][:, cc * T1 + tl * 128:
                                              cc * T1 + (tl + 1) * 128]
                            return e.tensor_copy(
                                dst, ps_tx[bank][:, q * 128:(q + 1) * 128])
                        p.add("vector", evac,
                              waits=[(s_tx0, s_tx0.n)] if q == 0 else [],
                              incs=[(s_txe, 1)] if last else [],
                              cincs=[(s_xnr, 1)]
                              if (last and last_of_chunk) else ())

            def emit_wfload(c):
                for sl in range(4):
                    p.add("sync",
                          lambda e, sl=sl: e.dma_start(
                              wfbuf[:, sl * 8 * FS:(sl + 1) * 8 * FS],
                              wfS[:, sl * 8 * FS:(sl + 1) * 8 * FS]),
                          waits=[(s_ffe, 2 * c)] if sl == 0 else [],
                          incs=[(s_wf, 16)])

            def emit_mm(c):
                b, tc = c // CPB, c % CPB
                t0 = tc * T1
                cpar = c % 2
                xn_ready = txe_base + 16 * (c + 1)
                for w in range(3):
                    for fc in range(4):
                        gidx = c * 12 + w * 4 + fc
                        bank = gidx % 4
                        for cc in range(32):
                            w0 = []
                            if cc == 0:
                                w0 = [(s_xnr, c + 1),
                                      (s_pje, max(0, gidx - 3))]
                            p.add("tensor",
                                  lambda e, w=w, fc=fc, cc=cc, bank=bank,
                                         cpar=cpar: e.matmul(
                                      ps_pj[bank][:, 0:T1],
                                      WT[w][:, cc * 512 + fc * 128:
                                            cc * 512 + (fc + 1) * 128],
                                      xn_fm[cpar][:, cc * T1:(cc + 1) * T1],
                                      start=(cc == 0), stop=(cc == 31)),
                                  waits=w0,
                                  incs=[(s_prj, 1)] if cc == 31 else ())
                        stg = gidx % 2
                        p.add("scalar",
                              lambda e, w=w, fc=fc, bank=bank, stg=stg:
                              e.activation(qstg[stg][:, :],
                                           ps_pj[bank][:, 0:T1], AF.Identity,
                                           bias=bprj[w][:, fc:fc + 1],
                                           scale=1.0),
                              waits=[(s_prj, gidx + 1),
                                     (s_scq, 16 * max(0, gidx - 1))],
                              incs=[(s_pje, 1)], cincs=[(s_pjd, 1)],
                              rb=qstg[stg][:, T1 - 1:T1])
                        p.add("gpsimd",
                              lambda e, w=w, fc=fc, b=b, t0=t0, stg=stg:
                              e.dma_start(
                                  SCR[w][fc // 2, b, fc % 2, :, t0:t0 + T1],
                                  qstg[stg][:, :]),
                              waits=[(s_pjd, gidx + 1)], incs=[(s_scq, 16)])
                for ts in range(2):
                    p.add("tensor",
                          lambda e, ts=ts: e.matmul(
                              ps_f[ts][:, :], idm[:, :], fb_bc[:, :],
                              start=True, stop=False),
                          waits=[(s_xnr, c + 1),
                                 (s_ffe, max(0, 2 * (c - 1) + ts + 1)),
                                 (s_wf, 64 * (c + 1))])
                    for cc in range(32):
                        p.add("tensor",
                              lambda e, cc=cc, ts=ts, cpar=cpar: e.matmul(
                                  ps_f[ts][:, :],
                                  xn_fm[cpar][:, cc * T1 + ts * 128:
                                              cc * T1 + (ts + 1) * 128],
                                  wfbuf[:, cc * FS:(cc + 1) * FS],
                                  start=False, stop=(cc == 31)),
                              incs=[(s_ffn, 1)] if cc == 31 else ())
                for ts in range(2):
                    fseq = 2 * c + ts
                    p.add("scalar",
                          lambda e, ts=ts: e.activation(gstg[ts][:, :],
                                                        ps_f[ts][:, :],
                                                        AF.Silu),
                          waits=[(s_ffn, fseq + 1),
                                 (s_scg, 16 * max(0, fseq - 1))],
                          cincs=[(s_sil, 1), (s_ffe, 1)],
                          rb=gstg[ts][:, FS - 1:FS])
                    p.add("gpsimd",
                          lambda e, ts=ts, b=b, t0=t0: e.dma_start(
                              gS[b, t0 + ts * 128:t0 + (ts + 1) * 128, :],
                              gstg[ts][:, :]),
                          waits=[(s_sil, fseq + 1)], incs=[(s_scg, 16)])

            nt1 = NT1 if p1_cap is None else p1_cap
            for c in range(nt1 + 2):
                if c < nt1:
                    emit_xload(2 * c)
                    emit_xload(2 * c + 1)
                if 1 <= c <= nt1:
                    emit_wfload(c - 1)
                    emit_prep(2 * (c - 1))
                    emit_prep(2 * (c - 1) + 1)
                    emit_prepB(2 * (c - 1))
                    emit_prepB(2 * (c - 1) + 1)
                if c >= 2:
                    emit_mm(c - 2)
            p.add("gpsimd", lambda e: e.nop(),
                  waits=[(s_scq, s_scq.n), (s_scg, s_scg.n)])
            p.add("sync", lambda e: e.nop(),
                  waits=[(s_xp[0], s_xp[0].n), (s_xp[1], s_xp[1].n),
                         (s_wf, s_wf.n)])
            if debug:
                for nm, S in (("dbg_q", qS), ("dbg_k", kS), ("dbg_v", vS)):
                    p.add("gpsimd", lambda e, nm=nm, S=S: e.dma_start(
                        dbg[nm], S[0, 0, :, :, :]), incs=[(s0, 16)])
                p.add("gpsimd", lambda e: e.dma_start(
                    dbg["dbg_g"], gS[0, 0:256, :]), incs=[(s0, 16)])
                p.add("gpsimd", lambda e: e.dma_start(
                    dbg["dbg_lnst"], lnst[:, 0:8]), incs=[(s0, 16)])
                p.add("gpsimd", lambda e: e.dma_start(
                    dbg["dbg_stats"], stats[:, :]), incs=[(s0, 16)])
                p.add("gpsimd", lambda e: e.dma_start(
                    dbg["dbg_xtm"], xn_tm[:, :]), incs=[(s0, 16)])
                p.add("gpsimd", lambda e: e.dma_start(
                    dbg["dbg_xnf"], xn_fm[0][:, :]), incs=[(s0, 16)])
                p.add("gpsimd", lambda e: e.nop(), waits=[(s0, s0.n)])
            p.run(nc, carriers)
        sA.__exit__(None, None, None)

        if nphases < 3:
            return nc
        with ExitStack() as sP2:
            qb = [sbuf(sP2, f"qb{i}", [128, 2 * L], BF16) for i in range(2)]
            kb = [sbuf(sP2, f"kb{i}", [128, 2 * L], BF16) for i in range(2)]
            vb = [sbuf(sP2, f"vb{i}", [128, 2 * L], BF16) for i in range(2)]
            vT = sbuf(sP2, "vT", [128, 32 * 257], BF16)
            logSB = [sbuf(sP2, f"logSB{i}", [128, L], F32) for i in range(2)]
            expSB = [sbuf(sP2, f"expSB{i}", [128, L], BF16) for i in range(2)]
            attnT = [sbuf(sP2, f"attnT{i}", [128, 32 * 128], BF16)
                     for i in range(2)]
            negmx = [sbuf(sP2, f"negmx{i}", [128, 1], F32) for i in range(2)]
            mx4 = sbuf(sP2, "mx4", [128, 4], F32)
            zcol = sbuf(sP2, "zcol", [128, 1], F32)
            denom = [sbuf(sP2, f"denom{i}", [128, 1], F32) for i in range(2)]
            recip = [sbuf(sP2, f"recip{i}", [128, 1], F32) for i in range(2)]
            sfb = [sbuf(sP2, f"sfb{i}", [128, HD], F32) for i in range(2)]
            sgb = [sbuf(sP2, f"sgb{i}", [128, HD], F32) for i in range(2)]
            gt = [sbuf(sP2, f"gt{i}", [128, HD], F32) for i in range(4)]
            xt = [sbuf(sP2, f"xt{i}", [128, HD], F32) for i in range(4)]
            ot = [sbuf(sP2, f"ot{i}", [128, HD], F32) for i in range(4)]
            ps_ga = psum(sP2, "ps_ga", [128, 1024])
            ps_gb = psum(sP2, "ps_gb", [128, 1024])
            ps_tp = [psum(sP2, f"ps_tp{i}", [128, 512]) for i in range(2)]
            ps_av = [psum(sP2, f"ps_av{i}", [128, 512]) for i in range(2)]

            pairs = [(b, h) for b in range(B) for h in range(HPC)][:npair]
            nlc = npair * LCH if nlc_cap is None else nlc_cap
            p = Phase("P2")
            tpe_after = []

            def emit_pair_load(j):
                b, h = pairs[j]
                par = j % 2
                for djc in range(2):
                    for buf, S in ((qb, qS), (kb, kS), (vb, vS)):
                        w0 = []
                        if j >= 2:
                            w0 = [(s_gae, 64 * (j - 1)),
                                  (s_vtr, 17 * (j - 1))]
                        p.add("sync",
                              lambda e, buf=buf, S=S, par=par, h=h, b=b,
                                     djc=djc: e.dma_start(
                                  buf[par][:, djc * L:(djc + 1) * L],
                                  S[h, b, djc, :, :]),
                              waits=w0, incs=[(s_pl, 16)])

            def emit_vtr(j):
                for vt in range(16):
                    bank = s_tpf.n % 2
                    for q in range(4):
                        col = vt * 512 + q * 128
                        mc, djc = col // 256, (col % 256) // 128
                        w0 = []
                        if q == 0:
                            w0 = [(s_tpe, max(0, s_tpf.n - 1)),
                                  (s_pl, 96 * (j + 1))]
                        p.add("tensor",
                              lambda e, bank=bank, q=q, mc=mc, djc=djc,
                                     par=j % 2: e.matmul(
                                  ps_tp[bank][:, q * 128:(q + 1) * 128],
                                  vb[par][:, djc * L + mc * 128:
                                          djc * L + (mc + 1) * 128],
                                  idm[:, :], start=True, stop=True),
                              waits=w0, incs=[(s_tpf, 1)] if q == 3 else ())
                    w1 = [(s_tpf, s_tpf.n)]
                    if vt == 0 and j >= 1:
                        w1.append((s_avd, 32 * j))

                    def vevac(e, vt=vt, bank=bank):
                        dst = vT[:, :].rearrange("p (mc c) -> p mc c", c=257)[
                            :, 2 * vt:2 * vt + 2, 0:256]
                        srcv = ps_tp[bank][:, :].rearrange(
                            "p (t c) -> p t c", c=256)
                        return e.tensor_copy(dst, srcv)
                    p.add("vector", vevac,
                          waits=w1, incs=[(s_tpe, 1)], cincs=[(s_vtr, 1)])
                p.add("vector",
                      lambda e: e.memset(
                          vT[:, :].rearrange("p (mc c) -> p mc c", c=257)[
                              :, :, 256:257], 1.0),
                      cincs=[(s_vtr, 1)])

            def emit_gxload(gi):
                j, li = gi // LCH, gi % LCH
                b, h = pairs[j]
                l0 = li * 128
                p.add("sync",
                      lambda e, b=b, h=h, l0=l0, r=gi % 4: e.dma_start(
                          gt[r][:, :],
                          gS[b, l0:l0 + 128, h * HD:(h + 1) * HD]),
                      waits=[(s_out, max(0, gi - 3))], incs=[(s_gx, 16)])
                p.add("sync",
                      lambda e, b=b, h=h, l0=l0, r=gi % 4: e.dma_start(
                          xt[r][:, :],
                          xres_d[b, l0:l0 + 128, h * HD:(h + 1) * HD]),
                      incs=[(s_gx, 16)])

            def emit_SA(gi):
                j, li = gi // LCH, gi % LCH
                par = j % 2
                gpar = gi % 2
                l0 = li * 128
                for rep in range(2):
                    for half, (ps_g, sf, se) in enumerate(
                            ((ps_ga, s_gaf, s_gae), (ps_gb, s_gbf, s_gbe))):
                        for sub in range(2):
                            mts = rep * 4 + half * 2 + sub
                            m0 = mts * 512
                            for djc in range(2):
                                w0 = []
                                if sub == 0 and djc == 0:
                                    w0 = [(se, sf.n)]
                                    if li == 0:
                                        w0.append((s_pl, 96 * (j + 1)))
                                inc = []
                                if sub == 1 and djc == 1:
                                    inc = [(sf, 1)]
                                p.add("tensor",
                                      lambda e, ps_g=ps_g, sub=sub, djc=djc,
                                             par=par, l0=l0, m0=m0: e.matmul(
                                          ps_g[:, sub * 512:(sub + 1) * 512],
                                          qb[par][:, djc * L + l0:
                                                  djc * L + l0 + 128],
                                          kb[par][:, djc * L + m0:
                                                  djc * L + m0 + 512],
                                          start=(djc == 0), stop=(djc == 1)),
                                      waits=w0, incs=inc)
                        seg = rep * 2 + half
                        w1 = [(sf, sf.n)]
                        if gi >= 2 and seg == 0:
                            w1.append((s_max, gi - 1))
                        p.add("vector",
                              lambda e, ps_g=ps_g, seg=seg: e.reduce_max(
                                  mx4[:, seg:seg + 1], ps_g[:, :], axis=AX.X),
                              waits=w1)
                        p.add("vector",
                              lambda e, ps_g=ps_g, gpar=gpar, seg=seg:
                              e.tensor_copy(logSB[gpar][:, seg * 1024:
                                                        (seg + 1) * 1024],
                                            ps_g[:, :]),
                              incs=[(se, 1)])

                p.add("vector",
                      lambda e, gpar=gpar: e.reduce_max(
                          negmx[gpar][:, :], mx4[:, 0:4], axis=AX.X,
                          negate=True),
                      waits=[(s_exp, max(0, gi - 1))],
                      incs=[], cincs=[(s_max, 1)])

            def emit_SB_tr(gi):
                gpar = gi % 2
                w0 = [(s_max, gi + 1)]
                if gi >= 2:
                    w0.append((s_tpe, tpe_after[gi - 2]))
                p.add("scalar",
                      lambda e, gpar=gpar: e.activation(
                          expSB[gpar][:, :], logSB[gpar][:, :], AF.Exp,
                          bias=negmx[gpar][:, :], scale=1.0),
                      waits=w0, cincs=[(s_exp, 1)])
                for r in range(8):
                    bank = s_tpf.n % 2
                    for q in range(4):
                        mc = r * 4 + q
                        w1 = []
                        if q == 0:
                            w1 = [(s_tpe, max(0, s_tpf.n - 1)),
                                  (s_exp, gi + 1)]
                        inc = [(s_tpf, 1)] if q == 3 else []
                        p.add("tensor",
                              lambda e, bank=bank, q=q, mc=mc, gpar=gpar:
                              e.matmul(
                                  ps_tp[bank][:, q * 128:(q + 1) * 128],
                                  expSB[gpar][:, mc * 128:(mc + 1) * 128],
                                  idm[:, :], start=True, stop=True),
                              waits=w1, incs=inc)
                    w2 = [(s_tpf, s_tpf.n)]
                    if gi >= 2:
                        w2.append((s_avd, gi - 1))
                    p.add("vector",
                          lambda e, r=r, bank=bank, gpar=gpar: e.tensor_copy(
                              attnT[gpar][:, r * 512:(r + 1) * 512],
                              ps_tp[bank][:, :]),
                          waits=w2, incs=[(s_tpe, 1)],
                          cincs=[(s_atr, 1)] if r == 7 else ())
                tpe_after.append(s_tpe.n)

            def emit_SB_av(gi):
                gpar = gi % 2
                j = gi // LCH
                for mc in range(32):
                    w3 = []
                    if mc == 0:
                        w3 = [(s_atr, gi + 1), (s_vtr, 17 * (j + 1)),
                              (s_sn, max(0, gi - 1))]
                    inc = []
                    if mc == 31:
                        inc = [(s_avd, 1)]
                    p.add("tensor",
                          lambda e, mc=mc, gpar=gpar: e.matmul(
                              ps_av[gpar][:, 0:HD + 1],
                              attnT[gpar][:, mc * 128:(mc + 1) * 128],
                              vT[:, mc * 257:(mc + 1) * 257],
                              start=(mc == 0), stop=(mc == 31)),
                          waits=w3, incs=inc)

            def emit_SC(gi):
                j, li = gi // LCH, gi % LCH
                b, h = pairs[j]
                gpar = gi % 2
                r4 = gi % 4
                l0 = li * 128
                if gi + 1 < nlc:
                    ngpar = (gi + 1) % 2
                    p.add("vector",
                          lambda e, ngpar=ngpar: e.reciprocal(
                              recip[ngpar][:, :], ps_av[ngpar][:, HD:HD + 1]),
                          waits=[(s_avd, gi + 2)])
                p.add("vector",
                      lambda e, gpar=gpar: e.tensor_tensor(
                          sfb[gpar][:, :], ps_av[gpar][:, 0:HD],
                          recip[gpar][:, :].broadcast_to([128, HD]),
                          ALU.mult),
                      waits=[(s_avd, gi + 1)],
                      incs=[(s_sn, 1)])
                p.add("vector",
                      lambda e, gpar=gpar, r4=r4: e.tensor_mul(
                          sgb[gpar][:, :], sfb[gpar][:, :], gt[r4][:, :]),
                      waits=[(s_gx, 32 * (gi + 1))])
                p.add("vector",
                      lambda e, gpar=gpar, r4=r4: e.tensor_add(
                          ot[r4][:, :], sgb[gpar][:, :], xt[r4][:, :]),
                      waits=[(s_ost, 16 * max(0, gi - 3))],
                      cincs=[(s_out, 1)])
                p.add("gpsimd",
                      lambda e, b=b, h=h, l0=l0, r4=r4: e.dma_start(
                          out_d[b, l0:l0 + 128, h * HD:(h + 1) * HD],
                          ot[r4][:, :]),
                      waits=[(s_out, gi + 1)], incs=[(s_ost, 16)])

            p.add("vector", lambda e: e.memset(zcol[:, :], 0.0))
            emit_pair_load(0)
            for gi in range(nlc + 2):
                j, li = gi // LCH, gi % LCH
                if 1 <= gi <= nlc:
                    emit_SB_tr(gi - 1)
                if gi < nlc:
                    if li == 16 and j + 1 < npair:
                        emit_pair_load(j + 1)
                    emit_gxload(gi)
                    emit_SA(gi)
                if 1 <= gi <= nlc:
                    emit_SB_av(gi - 1)
                if gi == 1:
                    p.add("vector",
                          lambda e: e.reciprocal(
                              recip[0][:, :], ps_av[0][:, HD:HD + 1]),
                          waits=[(s_avd, 1)])
                if gi < nlc and li == 0:
                    emit_vtr(j)
                if gi >= 2:
                    emit_SC(gi - 2)
            p.add("gpsimd", lambda e: e.nop(), waits=[(s_ost, s_ost.n)])
            p.add("sync", lambda e: e.nop(),
                  waits=[(s_pl, s_pl.n), (s_gx, s_gx.n)])
            if debug:
                for nm, src_ap in (("dbg_log", logSB[0][:, :]),
                                   ("dbg_exp", expSB[0][:, :]),
                                   ("dbg_nm", negmx[0][:, :]),
                                   ("dbg_dn", recip[0][:, :]),
                                   ("dbg_sf", sfb[0][:, :]),
                                   ("dbg_vt", vT[:, :]),
                                   ("dbg_at", attnT[0][:, :])):
                    p.add("gpsimd", lambda e, nm=nm, s_=src_ap: e.dma_start(
                        dbg[nm], s_), incs=[(s0, 16)])
                p.add("gpsimd", lambda e: e.nop(), waits=[(s0, s0.n)])
            p.run(nc, carriers)

    return nc


def kernel(L0_attn_x, L0_attn_Wq, L0_attn_bq, L0_attn_Wk, L0_attn_bk,
           L0_attn_Wv, L0_attn_bv, L0_attn_bias_weight, L0_ffn_qweight,
           L0_ffn_scales, L0_ffn_zeros, L0_ffn_bias):
    x = np.ascontiguousarray(np.asarray(L0_attn_x, dtype=np.float32))
    Wq = np.asarray(L0_attn_Wq, dtype=np.float32)
    Wk = np.asarray(L0_attn_Wk, dtype=np.float32)
    Wv = np.asarray(L0_attn_Wv, dtype=np.float32)
    bq = np.asarray(L0_attn_bq, dtype=np.float32)
    bk = np.asarray(L0_attn_bk, dtype=np.float32)
    bv = np.asarray(L0_attn_bv, dtype=np.float32)
    bw = np.asarray(L0_attn_bias_weight, dtype=np.float32)
    qw = np.asarray(L0_ffn_qweight, dtype=np.int32)
    sc = np.asarray(L0_ffn_scales, dtype=np.float32)
    zr = np.asarray(L0_ffn_zeros, dtype=np.float32)
    fb = np.asarray(L0_ffn_bias, dtype=np.float32)

    nc = build_program()
    in_maps = []
    for c in range(NCORES):
        f0, f1 = FS * c, FS * (c + 1)
        in_maps.append({
            "x": x,
            "xres": np.ascontiguousarray(x[:, :, f0:f1]),
            "wq": np.ascontiguousarray(Wq[f0:f1, :]),
            "wk": np.ascontiguousarray(Wk[f0:f1, :]),
            "wv": np.ascontiguousarray(Wv[f0:f1, :]),
            "bq": np.ascontiguousarray(bq[f0:f1]),
            "bk": np.ascontiguousarray(bk[f0:f1]),
            "bv": np.ascontiguousarray(bv[f0:f1]),
            "bwT": np.ascontiguousarray(bw[HPC * c:HPC * (c + 1), :].T),
            "qw": np.ascontiguousarray(qw[:, f0:f1]),
            "scales": np.ascontiguousarray(sc[f0:f1, 0]),
            "zeros": np.ascontiguousarray(zr[f0:f1, 0]),
            "fbias": np.ascontiguousarray(fb[f0:f1]),
            "bivkt": _BIVKT,
            "rev64": _REV64,
            "d64": _D64,
            "idm": _IDENT,
            "shifts": _SHIFTS,
        })
    res = run_bass_kernel_spmd(nc, in_maps, core_ids=list(range(NCORES)))
    return np.concatenate([np.asarray(r["out"]) for r in res.results], axis=2)
